# version 12
# baseline (speedup 1.0000x reference)
import math
import contextlib
import numpy as np
import ml_dtypes

import concourse.bass as bass
import concourse.mybir as mybir
from concourse.bass_utils import run_bass_kernel_spmd

F32 = mybir.dt.float32
BF16 = mybir.dt.bfloat16
U32 = mybir.dt.uint32
AF = mybir.ActivationFunctionType
ALU = mybir.AluOpType
AX = mybir.AxisListType

NCORES = 8
B, T, D = 2, 4096, 2048
TPC, CH, NTT, NKT = 1024, 512, 8, 16
NH, NKV, HD = 16, 4, 128
NEG = -1.0e30


class Cfg:
    def __init__(self, **kw):
        self.NE = 16
        self.PE_PAIRS = 11
        self.PS_PAIRS = 22
        self.W2KT = 11
        self.TOPK = 6
        self.P_A = 21
        self.stage = "l1"
        self.do_conv = True
        self.do_ffn1 = True
        self.do_attn = True
        self.do_ffn2 = True
        for k, v in kw.items():
            setattr(self, k, v)
        self.N_ET = 32 * self.PE_PAIRS + 16 * self.W2KT
        assert self.N_ET % NCORES == 0
        self.ET_PC = self.N_ET // NCORES
        self.MISC_SW13 = 0
        self.MISC_SW2 = 32 * self.PS_PAIRS
        self.MISC_WQ = self.MISC_SW2 + 32 * self.W2KT
        self.MISC_WK = self.MISC_WQ + 256
        self.MISC_WV = self.MISC_WK + 64
        self.MISC_WO = self.MISC_WV + 64
        self.N_MT = self.MISC_WO + 256
        assert self.N_MT % NCORES == 0
        self.MT_PC = self.N_MT // NCORES


def _expert_tile_src(cfg, t):
    n13 = 32 * cfg.PE_PAIRS
    if t < n13:
        p, r = divmod(t, 32)
        kt, gu = divmod(r, 2)
        return ("w13", 128 * p + (0 if gu == 0 else 128 * cfg.PE_PAIRS), 128 * kt)
    t -= n13
    n, r = divmod(t, 4 * cfg.W2KT)
    kt, s = divmod(r, 4)
    return ("w2", 512 * n + 128 * s, 128 * kt)


def _misc_tile_src(cfg, t):
    if t < 32 * cfg.PS_PAIRS:
        p, r = divmod(t, 32)
        kt, gu = divmod(r, 2)
        return ("sw13", 128 * p + (0 if gu == 0 else 128 * cfg.PS_PAIRS), 128 * kt)
    t -= 32 * cfg.PS_PAIRS
    if t < 32 * cfg.W2KT:
        n, r = divmod(t, 8 * cfg.W2KT)
        kh, r2 = divmod(r, 4 * cfg.W2KT)
        kt, s = divmod(r2, 4)
        return ("sw2", 512 * n + 128 * s, 128 * (cfg.W2KT * kh + kt))
    t -= 32 * cfg.W2KT
    if t < 256:
        ht, kt = divmod(t, 16)
        return ("Wq", 128 * ht, 128 * kt)
    t -= 256
    if t < 64:
        kvh, kt = divmod(t, 16)
        return ("Wk", 128 * kvh, 128 * kt)
    t -= 64
    if t < 64:
        kt, s = divmod(t, 4)
        return ("Wv", 128 * s, 128 * kt)
    t -= 64
    n, r = divmod(t, 64)
    hh, r2 = divmod(r, 32)
    h, s = divmod(r2, 4)
    return ("Wo", 512 * n + 128 * s, 128 * (8 * hh + h))


def _tile_segs(a, b, per_rank):
    return [(0, a, b - a)]


def _all_tiles(cfg):
    tiles = []
    for e in range(cfg.NE):
        for t in range(cfg.N_ET):
            tiles.append((e, t))
    for t in range(cfg.N_MT):
        tiles.append(("misc", t))
    return tiles



def _plan(rows):
    per_core = []
    for c in range(NCORES):
        b, j = divmod(c, 4)
        blocks = set(range(8 * j, 8 * j + 8)) | set(range(8 * (7 - j), 8 * (7 - j) + 8))
        groups = [g for g in range(128) if int(rows[g]) in blocks]
        per_core.append((b, j, blocks, groups))
    pairs_pc = []
    for (b, j, blocks, groups) in per_core:
        buckets = {}
        for g in groups:
            buckets.setdefault(int(rows[g]), []).append(g)
        pairs = []
        while True:
            order = [r for r in sorted(buckets, key=lambda r: -len(buckets[r]))
                     if buckets[r]]
            if not order:
                break
            if len(order) == 1:
                pairs.append((buckets[order[0]].pop(), None))
            else:
                pairs.append((buckets[order[0]].pop(), buckets[order[1]].pop()))
        pairs_pc.append(pairs)
    P_A = max(1, max(len(p) for p in pairs_pc))
    return per_core, pairs_pc, P_A


def _local_row(j, rb):
    lc = 0 if rb // 8 == j else 1
    return 512 * lc + 64 * (rb % 8)


def _stage_core(cfg, c, plan, x, bv, cols, rows, Wq, Wk, Wv, Wo, q_scale,
                k_scale, gate_w, w13, w2, sw13, sw2):
    per_core, pairs_pc, P_A = plan
    b, j, blocks, groups = per_core[c]
    pairs = pairs_pc[c]
    d = {}
    d["x_shard"] = np.ascontiguousarray(np.stack(
        [x[b, 512 * j:512 * j + 512], x[b, 512 * (7 - j):512 * (7 - j) + 512]]))
    csrc = np.zeros((cfg.P_A, 2, 64, D), np.float32)
    cbv = np.zeros((cfg.P_A, 2, 16, 64, 64), np.float32)
    cidx = np.tile((TPC + np.arange(128, dtype=np.uint32))[:, None],
                   (1, cfg.P_A))
    for s, pr in enumerate(pairs):
        for half, g in enumerate(pr):
            if g is None:
                continue
            cb = int(cols[g])
            csrc[s, half] = x[b, 64 * cb:64 * cb + 64, :]
            cbv[s, half] = np.transpose(bv[g], (2, 0, 1))
            base = _local_row(j, int(rows[g]))
            cidx[64 * half:64 * half + 64, s] = base + np.arange(64, dtype=np.uint32)
    d["conv_src"] = csrc
    d["conv_bv"] = cbv
    d["conv_idx"] = cidx
    d["w13f"] = w13
    d["w2f"] = w2
    d["sw13f"] = sw13
    d["sw2f"] = sw2
    d["Wqf"] = Wq
    d["Wkf"] = Wk
    d["Wvf"] = Wv
    d["Wof"] = Wo
    pos = (j, 7 - j)
    m2 = np.zeros((2, 8, 4, 128, 512), np.float32)
    for ci in range(2):
        qglob = 512 * pos[ci] + np.arange(512)[None, :]
        for u in range(8):
            for kt in range(4):
                kglob = (512 * u + 128 * kt) + np.arange(128)[:, None]
                m2[ci, u, kt] = (kglob <= qglob).astype(np.float32)
    d["amask2"] = m2.astype(ml_dtypes.bfloat16)
    d["ident"] = np.eye(128, dtype=np.float32)
    d["ones_col"] = np.ones((128, 1), np.float32)
    d["ones_row"] = np.ones((1, 128), np.float32)
    d["q_scale"] = np.ascontiguousarray(np.asarray(q_scale, np.float32).reshape(128, 1))
    d["k_scale"] = np.ascontiguousarray(np.asarray(k_scale, np.float32).reshape(128, 1))
    d["gate_src"] = np.ascontiguousarray(gate_w)
    fc = np.zeros((128, 3), np.float32)
    fc[:, 0] = 1e-5
    fc[:, 1] = 1e-6
    fc[:, 2] = -30.0
    d["fconst"] = fc
    return d



class Env:
    pass


class RecInstr:
    __slots__ = ("incs",)

    def __init__(self):
        self.incs = []

    def then_inc(self, sem, n):
        self.incs.append((sem, n))
        return self


class Rec:

    def __init__(self):
        self.ops = []

    def __getattr__(self, name):
        def call(*a, **k):
            r = RecInstr()
            self.ops.append((name, a, k, r))
            return r
        return call


def _replay(env, engname):
    ops = env.rec[engname].ops
    CH = 6000
    for lo in range(0, len(ops), CH):
        chunk = ops[lo:lo + CH]

        def run(eng, chunk=chunk):
            for (name, a, k, r) in chunk:
                instr = getattr(eng, name)(*a, **k)
                for (sem, n) in r.incs:
                    instr = instr.then_inc(sem, n)
        getattr(env.block, engname)(run)


def _replay_all(env):
    for engname in ("tensor", "vector", "scalar", "sync", "gpsimd"):
        _replay(env, engname)


def _mk_sems(env, names):
    for n in names:
        env.hdl[n] = env.stack.enter_context(env.nc.semaphore(n))
        env.val[n] = 0


def _tok(env, name):
    return (name, env.val[name])


def _bump(env, instr, name, k=1):
    instr.then_inc(env.hdl[name], k)
    env.val[name] += k
    return (name, env.val[name])


def _wait(env, eng, engname, tok):
    if tok is None:
        return
    name, v = tok
    if v <= 0:
        return
    key = (engname, name)
    if env.waited.get(key, 0) >= v:
        return
    eng.wait_ge(env.hdl[name], v)
    env.waited[key] = v


def _wait_many(env, eng, engname, toks):
    for t in toks:
        _wait(env, eng, engname, t)


def _sec(env, engname, fn):
    fn(env.rec[engname])


def _phase_barrier_toks(env):
    return [_tok(env, "s_pe"), _tok(env, "s_dve"), _tok(env, "s_act"),
            _tok(env, "d_sc"), _tok(env, "d_y")]



def _emit_ln(env):
    tok_x_ready = env.tok_x_ready
    tok_xlt_free = env.tok_xlt_free
    ev_last = [None]
    for tt in range(NTT):
        def _v(v, tt=tt):
            _wait(env, v, "dve", tok_x_ready)
            xt = env.x_sb[:, D * tt:D * tt + D]
            cw = env.colw
            v.reduce_sum(cw[:, 0:1], xt, axis=AX.X)
            _bump(env, v.engine_nop(), "s_dve") if False else None
            env.val["s_dve"] += 0
            v.then_inc_placeholder = None
        _sec(env, "vector", lambda v: None)

        def _s0(s, tt=tt):
            _wait(env, s, "act", tok_x_ready)
            _wait(env, s, "act", env.t_sqs_free)
            _bump(env, s.activation(env.sqs[:, :],
                                    env.x_sb[:, D * tt:D * tt + D],
                                    AF.Square), "s_act")
        _sec(env, "scalar", _s0)
        t_sq0 = _tok(env, "s_act")

        def _v1(v, tt=tt):
            _wait(env, v, "dve", tok_x_ready)
            xt = env.x_sb[:, D * tt:D * tt + D]
            cw = env.colw
            _bump(env, v.reduce_sum(cw[:, 0:1], xt, axis=AX.X), "s_dve")
            _wait(env, v, "dve", t_sq0)
            _bump(env, v.reduce_sum(cw[:, 1:2], env.sqs[:, :], axis=AX.X),
                  "s_dve")
            env.t_sqs_free = _tok(env, "s_dve")
            _wait(env, v, "dve", _tok(env, "s_dve"))
            _bump(env, v.tensor_tensor(cw[:, 2:3], cw[:, 0:1], cw[:, 0:1],
                                       ALU.mult), "s_dve")
            _wait(env, v, "dve", _tok(env, "s_dve"))
            _bump(env, v.scalar_tensor_tensor(cw[:, 3:4], cw[:, 2:3], -1.0 / D,
                                              cw[:, 1:2], ALU.mult, ALU.add),
                  "s_dve")
        _sec(env, "vector", _v1)
        t_var = _tok(env, "s_dve")

        def _s(s, tt=tt):
            _wait(env, s, "act", t_var)
            _wait(env, s, "act", env.tok_cst)
            _bump(env, s.activation(env.colw[:, 4:5], env.colw[:, 3:4],
                                    AF.Sqrt, bias=env.cst[:, 0:1],
                                    scale=1.0 / D), "s_act")
        _sec(env, "scalar", _s)
        t_sqrt = _tok(env, "s_act")

        def _v2(v, tt=tt):
            _wait(env, v, "dve", t_sqrt)
            cw = env.colw
            _bump(env, v.reciprocal(cw[:, 5:6], cw[:, 4:5]), "s_dve")
            _wait(env, v, "dve", _tok(env, "s_dve"))
            _bump(env, v.tensor_scalar_mul(cw[:, 6:7], cw[:, 0:1], 1.0 / D),
                  "s_dve")
            _wait(env, v, "dve", _tok(env, "s_dve"))
            _wait(env, v, "dve", env.t_xls_free)
            _bump(env, v.tensor_scalar(env.xls[:, :],
                                       env.x_sb[:, D * tt:D * tt + D],
                                       cw[:, 6:7], cw[:, 5:6], ALU.subtract,
                                       ALU.mult), "s_dve")
        _sec(env, "vector", _v2)
        t_xls = _tok(env, "s_dve")

        def _t(t, tt=tt):
            _wait(env, t, "pe", t_xls)
            _wait(env, t, "pe", env.tok_smallbf)
            if tt == 0:
                _wait(env, t, "pe", tok_xlt_free)
                _wait_many(env, t, "pe", env.barrier_pe)
            _wait(env, t, "pe", ev_last[0])
            for kt in range(NKT):
                bk = env.ps[6 + (kt // 8)]
                _bump(env, t.transpose(
                    bk[:, :].bitcast(BF16)[:, 128 * (kt % 8):128 * (kt % 8) + 128],
                    env.xls[:, 128 * kt:128 * kt + 128],
                    env.identbf[:, :]), "s_pe")
        _sec(env, "tensor", _t)
        env.t_xls_free = _tok(env, "s_pe")
        t_tr = _tok(env, "s_pe")

        def _v3(v, tt=tt):
            _wait(env, v, "dve", t_tr)
            for half in range(2):
                bk = env.ps[6 + half]
                out = env.xLT[:, :].rearrange("p (k q) -> p k q", k=NKT)[
                    :, 8 * half:8 * half + 8, 128 * tt:128 * tt + 128]
                src = bk[:, :].bitcast(BF16)[:, 0:1024].rearrange(
                    "p (k q) -> p k q", k=8)
                _bump(env, v.tensor_copy(out, src), "s_dve")
            ev_last[0] = _tok(env, "s_dve")
        _sec(env, "vector", _v3)
    env.tok_xlt_ready = _tok(env, "s_dve")



def _emit_gating(env):
    cfg = env.cfg
    NEx = cfg.NE
    g_toks = []
    for tt in range(NTT):
        def _t(t, tt=tt):
            _wait(env, t, "pe", env.tok_xlt_ready)
            _wait(env, t, "pe", env.tok_gateT)
            bk = env.ps[6 + tt % 2]
            _wait(env, t, "pe", g_toks[-2] if len(g_toks) >= 2 else None)
            mm = None
            for kt in range(NKT):
                mm = t.matmul(bk[0:128, 0:NEx],
                              env.xLT[:, TPC * kt + 128 * tt:TPC * kt + 128 * tt + 128],
                              env.gateT[:, 16 * kt:16 * kt + NEx],
                              start=(kt == 0), stop=(kt == NKT - 1))
            _bump(env, mm, "s_pe")
        _sec(env, "tensor", _t)
        t_log = _tok(env, "s_pe")

        def _v(v, tt=tt):
            bk = env.ps[6 + tt % 2]
            cw = env.colw
            _wait(env, v, "dve", t_log)
            _bump(env, v.reduce_max(cw[:, 8:9], bk[0:128, 0:NEx], axis=AX.X,
                                    negate=True), "s_dve")
        _sec(env, "vector", _v)
        t_negm = _tok(env, "s_dve")

        def _s(s, tt=tt):
            bk = env.ps[6 + tt % 2]
            _wait(env, s, "act", t_negm)
            _bump(env, s.activation(env.gwork[:, 0:NEx], bk[0:128, 0:NEx],
                                    AF.Exp, bias=env.colw[:, 8:9]), "s_act")
        _sec(env, "scalar", _s)
        t_e = _tok(env, "s_act")

        def _v2(v, tt=tt):
            cw = env.colw
            gw = env.gwork
            _wait(env, v, "dve", t_e)
            wa, wb = gw[:, 16:16 + NEx], gw[:, 32:32 + NEx]
            _bump(env, v.tensor_copy(wa, gw[:, 0:NEx]), "s_dve")
            bufs = [wa, wb]
            k = min(cfg.TOPK, NEx)
            eqb = gw[:, 32 + 16:32 + 16 + NEx] if False else None
            for r in range(k):
                _wait(env, v, "dve", _tok(env, "s_dve"))
                _bump(env, v.reduce_max(cw[:, 9 + r:10 + r], bufs[r % 2],
                                        axis=AX.X), "s_dve")
                _wait(env, v, "dve", _tok(env, "s_dve"))
                _bump(env, v.tensor_single_scalar(cw[:, 17:17 + NEx],
                                                  bufs[r % 2],
                                                  cw[:, 9 + r:10 + r],
                                                  ALU.is_equal), "s_dve")
                _wait(env, v, "dve", _tok(env, "s_dve"))
                _bump(env, v.scalar_tensor_tensor(bufs[(r + 1) % 2],
                                                  cw[:, 17:17 + NEx], NEG,
                                                  bufs[r % 2], ALU.mult,
                                                  ALU.add), "s_dve")
            thr = cw[:, 9 + k - 1:10 + k - 1]
            _wait(env, v, "dve", _tok(env, "s_dve"))
            _bump(env, v.tensor_single_scalar(gw[:, 16:16 + NEx], gw[:, 0:NEx],
                                              thr, ALU.is_ge), "s_dve")
            _wait(env, v, "dve", _tok(env, "s_dve"))
            _bump(env, v.tensor_tensor(gw[:, 32:32 + NEx], gw[:, 0:NEx],
                                       gw[:, 16:16 + NEx], ALU.mult), "s_dve")
            _wait(env, v, "dve", _tok(env, "s_dve"))
            _bump(env, v.reduce_sum(cw[:, 15:16], gw[:, 32:32 + NEx],
                                    axis=AX.X), "s_dve")
            _wait(env, v, "dve", _tok(env, "s_dve"))
            _bump(env, v.reciprocal(cw[:, 16:17], cw[:, 15:16]), "s_dve")
            _wait(env, v, "dve", _tok(env, "s_dve"))
            _bump(env, v.tensor_scalar_mul(env.combine[:, 16 * tt:16 * tt + NEx],
                                           gw[:, 32:32 + NEx], cw[:, 16:17]),
                  "s_dve")
            g_toks.append(_tok(env, "s_dve"))
        _sec(env, "vector", _v2)
    env.tok_combT = _tok(env, "s_dve")


def _emit_cb(env, e):
    ex_toks = []
    for tt in range(NTT):
        def _t(t, e=e, tt=tt):
            _wait(env, t, "pe", env.tok_combT)
            _wait(env, t, "pe", env.cb_free.get(e % 2))
            _wait(env, t, "pe", env.cb_prev_v2)
            _wait(env, t, "pe", ex_toks[-2] if len(ex_toks) >= 2 else None)
            bk = env.ps[6 + tt % 2]
            _bump(env, t.matmul(bk[0:1, 0:128],
                                env.combine[:, 16 * tt + e:16 * tt + e + 1],
                                env.ident32[:, :], start=True, stop=True),
                  "s_pe")
        _sec(env, "tensor", _t)
        t_mm = _tok(env, "s_pe")

        def _v(v, tt=tt, t_mm=t_mm):
            _wait(env, v, "dve", t_mm)
            bk = env.ps[6 + tt % 2]
            _bump(env, v.tensor_copy(env.nrow[0:1, 128 * tt:128 * tt + 128],
                                     bk[0:1, 0:128]), "s_dve")
            ex_toks.append(_tok(env, "s_dve"))
        _sec(env, "vector", _v)
    t_row = _tok(env, "s_dve")

    def _t2(t, e=e):
        _wait(env, t, "pe", t_row)
        for ch in range(2):
            bk = env.ps[6 + ch]
            _bump(env, t.matmul(bk[0:128, 0:512], env.onesr32[:, :],
                                env.nrow[0:1, 512 * ch:512 * ch + 512],
                                start=True, stop=True), "s_pe")
    _sec(env, "tensor", _t2)
    t_mm2 = _tok(env, "s_pe")

    def _v2(v, e=e):
        _wait(env, v, "dve", t_mm2)
        for ch in range(2):
            bk = env.ps[6 + ch]
            _bump(env, v.tensor_copy(
                env.cb[:, TPC * (e % 2) + 512 * ch:TPC * (e % 2) + 512 * ch + 512],
                bk[0:128, 0:512]), "s_dve")
    _sec(env, "vector", _v2)
    env.cb_ready[e] = _tok(env, "s_dve")
    env.cb_prev_v2 = _tok(env, "s_dve")



def _emit_ffn(env, call):
    cfg = env.cfg
    st = env.ffn_state
    _emit_gating(env)
    env.cb_ready = {}
    env.cb_free = {}
    env.cb_prev_v2 = None
    _emit_cb(env, 0)
    if cfg.NE > 1:
        _emit_cb(env, 1)
    barrier_dve = list(env.barrier_dve)
    barrier_pe = list(env.barrier_pe)

    env.tok_x_ci0 = None
    for e in range(cfg.NE + 1):
        shared = (e == cfg.NE)
        npairs = cfg.PS_PAIRS if shared else cfg.PE_PAIRS
        chunk_passes = [[0], [1]] if shared else [[0, 1]]
        for pass_chunks in chunk_passes:
            if shared and pass_chunks == [1]:
                env.tok_x_ci0 = env.ffn_state["xadd_done"].get(
                    env.ffn_state["gm"] - 1)
            for p in range(npairs):
                gp = st["gp"]
                st["gp"] += 1
                buf = gp % 2
                ldsem = "d_pbA" if buf == 0 else "d_pbB"

                def _ld(s, e=e, p=p, gp=gp, buf=buf, ldsem=ldsem, shared=shared):
                    _wait(env, s, "sync", env.tok_cc["misc" if shared else e])
                    _wait(env, s, "sync", st["pb_use"].get(gp - 2))
                    if shared:
                        segs = _tile_segs(32 * p, 32 * p + 32, cfg.MT_PC)
                        src = env.wt_misc
                    else:
                        segs = _tile_segs(32 * p, 32 * p + 32, cfg.ET_PC)
                        src = env.wt_all[e]
                    ofs = 0
                    for (r, t0, n) in segs:
                        _bump(env, s.dma_start(
                            out=env.w13buf[:, 4096 * buf + 128 * ofs:
                                           4096 * buf + 128 * (ofs + n)],
                            in_=src[128 * r:128 * r + 128,
                                    128 * t0:128 * (t0 + n)]), ldsem, 16)
                        ofs += n
                    st["pb_load"][gp] = _tok(env, ldsem)
                _sec(env, "sync", _ld)

                for ch in pass_chunks:
                    gu = st["gu"]
                    st["gu"] += 1
                    par = gu % 2
                    psG, psU = env.ps[2 * par], env.ps[2 * par + 1]

                    def _t(t, e=e, p=p, ch=ch, gp=gp, gu=gu, buf=buf, par=par,
                           psG=psG, psU=psU, shared=shared,
                           first=(p == 0 and ch == pass_chunks[0])):
                        if first:
                            _wait(env, t, "pe", env.tok_cc["misc" if shared else e])
                            _wait(env, t, "pe", env.tok_xlt_ready)
                            _wait_many(env, t, "pe", barrier_pe)
                        _wait(env, t, "pe", st["pb_load"][gp])
                        _wait(env, t, "pe", st["silu_done"].get(gu - 2))
                        _wait(env, t, "pe", st["op1_done"].get(gu - 2))
                        mmG = mmU = None
                        for kt in range(NKT):
                            rhs = env.xLT[:, TPC * kt + 512 * ch:
                                          TPC * kt + 512 * ch + 512]
                            mmG = t.matmul(psG[:, :],
                                           env.w13buf[:, 4096 * buf + 256 * kt:
                                                      4096 * buf + 256 * kt + 128],
                                           rhs, start=(kt == 0),
                                           stop=(kt == NKT - 1))
                            mmU = t.matmul(psU[:, :],
                                           env.w13buf[:, 4096 * buf + 256 * kt + 128:
                                                      4096 * buf + 256 * kt + 256],
                                           rhs, start=(kt == 0),
                                           stop=(kt == NKT - 1))
                        _bump(env, mmG, "s_pe")
                        _bump(env, mmU, "s_pe")
                        st["unit_mm"][gu] = _tok(env, "s_pe")
                        if ch == pass_chunks[-1]:
                            st["pb_use"][gp] = _tok(env, "s_pe")
                    _sec(env, "tensor", _t)

                    def _s(s, gu=gu, par=par, psG=psG):
                        _wait(env, s, "act", st["unit_mm"][gu])
                        _wait(env, s, "act", st["op1_done"].get(gu - 2))
                        _bump(env, s.activation(
                            env.silu_t[:, 512 * par:512 * par + 512],
                            psG[:, :], AF.Silu), "s_act")
                        st["silu_done"][gu] = _tok(env, "s_act")
                    _sec(env, "scalar", _s)

                    def _v(v, e=e, p=p, ch=ch, gu=gu, par=par, psU=psU,
                           shared=shared):
                        _wait(env, v, "dve", st["silu_done"][gu])
                        _wait(env, v, "dve", st["unit_mm"][gu])
                        _wait(env, v, "dve", env.tok_h_free)
                        _wait_many(env, v, "dve", barrier_dve)
                        if shared:
                            _bump(env, v.tensor_tensor(
                                env.h_sb[:, 512 * p:512 * p + 512],
                                env.silu_t[:, 512 * par:512 * par + 512],
                                psU[:, :], ALU.mult), "s_dve")
                        else:
                            _bump(env, v.tensor_tensor(
                                env.silu_t[:, 512 * par:512 * par + 512],
                                env.silu_t[:, 512 * par:512 * par + 512],
                                psU[:, :], ALU.mult), "s_dve")
                            _wait(env, v, "dve", _tok(env, "s_dve"))
                            _wait(env, v, "dve", env.cb_ready[e])
                            _bump(env, v.tensor_tensor(
                                env.h_sb[:, TPC * p + 512 * ch:
                                         TPC * p + 512 * ch + 512],
                                env.silu_t[:, 512 * par:512 * par + 512],
                                env.cb[:, TPC * (e % 2) + 512 * ch:
                                       TPC * (e % 2) + 512 * ch + 512],
                                ALU.mult), "s_dve")
                        st["op1_done"][gu] = _tok(env, "s_dve")
                    _sec(env, "vector", _v)
            tok_h_ready = _tok(env, "s_dve")

            if (not shared) and (e + 2 <= cfg.NE - 1):
                env.cb_free[(e + 2) % 2] = st["op1_done"].get(st["gu"] - 1)
                _emit_cb(env, e + 2)

            w2kt = cfg.W2KT
            nch_list = ([(n, kh) for n in range(4) for kh in range(2)]
                        if shared else [(n, None) for n in range(4)])
            gns = {}

            def emit_w2_load(li):
                n, kh = nch_list[li]
                gn = st["gn"]
                st["gn"] += 1
                gns[(n, kh)] = gn
                buf = gn % 2
                ldsem = "d_w2A" if buf == 0 else "d_w2B"

                def _ld(s, e=e, n=n, kh=kh, gn=gn, buf=buf, ldsem=ldsem,
                        shared=shared):
                    _wait(env, s, "sync", env.tok_cc["misc" if shared else e])
                    _wait(env, s, "sync", st["w2_use"].get(gn - 2))
                    if shared:
                        a = cfg.MISC_SW2 + 2 * w2kt * 4 * n + w2kt * 4 * kh
                        segs = _tile_segs(a, a + 4 * w2kt, cfg.MT_PC)
                        src = env.wt_misc
                    else:
                        a = 32 * cfg.PE_PAIRS + 4 * w2kt * n
                        segs = _tile_segs(a, a + 4 * w2kt, cfg.ET_PC)
                        src = env.wt_all[e]
                    ofs = 0
                    for (r, t0, m) in segs:
                        _bump(env, s.dma_start(
                            out=env.w2buf[:, 5632 * buf + 128 * ofs:
                                          5632 * buf + 128 * (ofs + m)],
                            in_=src[128 * r:128 * r + 128,
                                    128 * t0:128 * (t0 + m)]), ldsem, 16)
                        ofs += m
                    st["w2_load"][gn] = _tok(env, ldsem)
                _sec(env, "sync", _ld)

            emit_w2_load(0)
            if len(nch_list) > 1:
                emit_w2_load(1)

            if shared:
                for n in range(4):
                    gn0, gn1 = gns[(n, 0)], gns[(n, 1)]
                    for tl in range(4):
                        tt = 4 * pass_chunks[0] + tl
                        gm = st["gm"]
                        st["gm"] += 1
                        bk = env.ps[4 + gm % 2]

                        def _t(t, n=n, tl=tl, tt=tt, gm=gm, bk=bk, gn0=gn0,
                               gn1=gn1):
                            _wait(env, t, "pe", tok_h_ready)
                            _wait(env, t, "pe", st["w2_load"][gn0])
                            _wait(env, t, "pe", st["w2_load"][gn1])
                            _wait(env, t, "pe", st["xadd_done"].get(gm - 2))
                            mm = None
                            for kh in range(2):
                                wb = (gn0 if kh == 0 else gn1) % 2
                                for kt in range(w2kt):
                                    mm = t.matmul(
                                        bk[:, :],
                                        env.h_sb[:, 512 * (w2kt * kh + kt) +
                                                 128 * tl:
                                                 512 * (w2kt * kh + kt) +
                                                 128 * tl + 128],
                                        env.w2buf[:, 5632 * wb + 512 * kt:
                                                  5632 * wb + 512 * kt + 512],
                                        start=(kh == 0 and kt == 0),
                                        stop=(kh == 1 and kt == w2kt - 1))
                            _bump(env, mm, "s_pe")
                            st["mm2_done"][gm] = _tok(env, "s_pe")
                            st["w2_use"][gn0] = _tok(env, "s_pe")
                            st["w2_use"][gn1] = _tok(env, "s_pe")
                        _sec(env, "tensor", _t)

                        def _v(v, n=n, tt=tt, gm=gm, bk=bk):
                            _wait(env, v, "dve", st["mm2_done"][gm])
                            xsl = env.x_sb[:, D * tt + 512 * n:
                                           D * tt + 512 * n + 512]
                            _bump(env, v.tensor_tensor(xsl, xsl, bk[:, :],
                                                       ALU.add), "s_dve")
                            st["xadd_done"][gm] = _tok(env, "s_dve")
                        _sec(env, "vector", _v)
                    for li in (2 * n + 2, 2 * n + 3):
                        if li < len(nch_list):
                            emit_w2_load(li)
            else:
                for n in range(4):
                    gn = gns[(n, None)]
                    for tt in range(NTT):
                        gm = st["gm"]
                        st["gm"] += 1
                        bk = env.ps[4 + gm % 2]

                        def _t(t, e=e, n=n, tt=tt, gm=gm, gn=gn, bk=bk):
                            _wait(env, t, "pe", tok_h_ready)
                            _wait(env, t, "pe", st["w2_load"][gn])
                            _wait(env, t, "pe", st["xadd_done"].get(gm - 2))
                            mm = None
                            for kt in range(w2kt):
                                mm = t.matmul(
                                    bk[:, :],
                                    env.h_sb[:, TPC * kt + 128 * tt:
                                             TPC * kt + 128 * tt + 128],
                                    env.w2buf[:, 5632 * (gn % 2) + 512 * kt:
                                              5632 * (gn % 2) + 512 * kt + 512],
                                    start=(kt == 0), stop=(kt == w2kt - 1))
                            _bump(env, mm, "s_pe")
                            st["mm2_done"][gm] = _tok(env, "s_pe")
                            if tt == NTT - 1:
                                st["w2_use"][gn] = _tok(env, "s_pe")
                        _sec(env, "tensor", _t)

                        def _v(v, n=n, tt=tt, gm=gm, bk=bk):
                            _wait(env, v, "dve", st["mm2_done"][gm])
                            xsl = env.x_sb[:, D * tt + 512 * n:
                                           D * tt + 512 * n + 512]
                            _bump(env, v.tensor_tensor(xsl, xsl, bk[:, :],
                                                       ALU.add), "s_dve")
                            st["xadd_done"][gm] = _tok(env, "s_dve")
                        _sec(env, "vector", _v)
                    if n + 2 < 4:
                        emit_w2_load(n + 2)
            env.tok_h_free = st["mm2_done"][st["gm"] - 1]
    env.tok_x_ready = st["xadd_done"][st["gm"] - 1]
    env.tok_xlt_free = st["mm2_done"][st["gm"] - 1]
    env.tok_wq_free = st["pb_use"][st["gp"] - 1]
    env.barrier_pe = [_tok(env, "s_pe")]
    env.barrier_dve = [_tok(env, "s_dve")]
    env.barrier_act = [_tok(env, "s_act")]



def _emit_proj_mm(env, bk, tok_w, wbuf_ap, co, extra_pe=None, bank_guard=None):
    def _t(t):
        _wait(env, t, "pe", tok_w)
        _wait(env, t, "pe", env.tok_xlt_ready)
        _wait(env, t, "pe", bank_guard)
        _wait(env, t, "pe", extra_pe)
        mm = None
        for kt in range(NKT):
            mm = t.matmul(bk[:, :], wbuf_ap[:, 128 * kt:128 * kt + 128],
                          env.xLT[:, TPC * kt + 512 * co:TPC * kt + 512 * co + 512],
                          start=(kt == 0), stop=(kt == NKT - 1))
        _bump(env, mm, "s_pe")
    _sec(env, "tensor", _t)
    return _tok(env, "s_pe")


def _emit_proj_tail(env, bk, t_mm, out_ap, scale_col, toks, extra_war=None):
    def _s(s):
        _wait(env, s, "act", t_mm)
        _wait(env, s, "act", toks.get(("sqsfree",)))
        _wait(env, s, "act", env.t_sqs_free)
        _bump(env, s.activation(env.sqs[:, 0:512], bk[:, :], AF.Square), "s_act")
    _sec(env, "scalar", _s)
    t_sq = _tok(env, "s_act")

    def _t2(t):
        _wait(env, t, "pe", t_sq)
        _wait(env, t, "pe", toks.get(("ps7free",)))
        sb7 = env.ps[7]
        _bump(env, t.matmul(sb7[0:1, :], env.onesc32[:, :], env.sqs[:, 0:512],
                            start=True, stop=True), "s_pe")
    _sec(env, "tensor", _t2)
    t_ss = _tok(env, "s_pe")
    toks[("sqsfree",)] = t_ss
    env.t_sqs_free = t_ss

    def _s2(s):
        sb7 = env.ps[7]
        _wait(env, s, "act", t_ss)
        _wait(env, s, "act", env.tok_cst)
        _wait(env, s, "act", toks.get(("nrowfree",)))
        _bump(env, s.activation(env.nrow[0:1, 0:512], sb7[0:1, :], AF.Sqrt,
                                bias=env.cst[0:1, 1:2], scale=1.0 / HD), "s_act")
    _sec(env, "scalar", _s2)
    t_rs0 = _tok(env, "s_act")
    toks[("ps7free",)] = t_rs0

    def _v15(v):
        _wait(env, v, "dve", t_rs0)
        _bump(env, v.reciprocal(env.nrow[0:1, 512:1024], env.nrow[0:1, 0:512]),
              "s_dve")
    _sec(env, "vector", _v15)
    t_rs = _tok(env, "s_dve")
    toks[("nrowfree",)] = t_rs

    def _t3(t):
        sb6 = env.ps[6]
        _wait(env, t, "pe", t_rs)
        _wait(env, t, "pe", toks.get(("ps6free",)))
        _bump(env, t.matmul(sb6[0:128, :], env.onesr32[:, :],
                            env.nrow[0:1, 512:1024], start=True, stop=True),
              "s_pe")
    _sec(env, "tensor", _t3)
    t_bc = _tok(env, "s_pe")

    def _v(v):
        sb6 = env.ps[6]
        _wait(env, v, "dve", t_bc)
        _wait(env, v, "dve", toks.get(("invbfree",)))
        _bump(env, v.tensor_copy(env.invb[:, :], sb6[0:128, :]), "s_dve")
        toks[("ps6free",)] = _tok(env, "s_dve")
        _wait(env, v, "dve", _tok(env, "s_dve"))
        _wait(env, v, "dve", extra_war)
        _bump(env, v.scalar_tensor_tensor(out_ap, bk[:, :], scale_col,
                                          env.invb[:, :], ALU.mult, ALU.mult),
              "s_dve")
        toks[("invbfree",)] = _tok(env, "s_dve")
    _sec(env, "vector", _v)
    return _tok(env, "s_dve")



def _emit_attn_kv(env):
    cfg = env.cfg
    st = env.att_state
    barrier_pe = list(env.barrier_pe)
    barrier_dve = list(env.barrier_dve)

    def _ldv(s):
        _wait(env, s, "sync", env.tok_cc["misc"])
        _wait(env, s, "sync", env.tok_wq_free)
        _wait_many(env, s, "sync", barrier_dve)
        segs = _tile_segs(cfg.MISC_WV, cfg.MISC_WV + 64, cfg.MT_PC)
        ofs = 0
        for (r, t0, n) in segs:
            _bump(env, s.dma_start(
                out=env.w13buf[:, 128 * ofs:128 * (ofs + n)],
                in_=env.wt_misc[128 * r:128 * r + 128,
                                128 * t0:128 * (t0 + n)]), "d_wqA", 16)
            ofs += n
    _sec(env, "sync", _ldv)
    t_wv = _tok(env, "d_wqA")

    kvw_last = [None]
    vst_free = {}
    for tt in range(NTT):
        bk = env.ps[tt % 2]

        def _t(t, tt=tt, bk=bk):
            _wait(env, t, "pe", env.tok_cc["misc"])
            _wait(env, t, "pe", t_wv)
            _wait(env, t, "pe", env.tok_xlt_ready)
            _wait_many(env, t, "pe", barrier_pe)
            _wait(env, t, "pe", vst_free.get(tt - 2))
            mm = None
            for kt in range(NKT):
                mm = t.matmul(bk[:, :],
                              env.xLT[:, TPC * kt + 128 * tt:
                                      TPC * kt + 128 * tt + 128],
                              env.w13buf[:, 512 * kt:512 * kt + 512],
                              start=(kt == 0), stop=(kt == NKT - 1))
            _bump(env, mm, "s_pe")
        _sec(env, "tensor", _t)
        t_vp = _tok(env, "s_pe")

        def _v(v, tt=tt, bk=bk):
            _wait(env, v, "dve", t_vp)
            _wait(env, v, "dve", kvw_last[0])
            _wait_many(env, v, "dve", barrier_dve)
            _bump(env, v.tensor_copy(env.vstage[:, :], bk[:, :]), "s_dve")
        _sec(env, "vector", _v)
        t_ev = _tok(env, "s_dve")
        vst_free[tt] = t_ev

        def _st(s, tt=tt):
            _wait(env, s, "sync", t_ev)
            for kvh in range(NKV):
                _bump(env, s.dma_start(
                    out=env.kvv_in[1024 * kvh + 128 * tt:
                                   1024 * kvh + 128 * tt + 128, :],
                    in_=env.vstage[:, 128 * kvh:128 * kvh + 128]), "d_kvw", 16)
            kvw_last[0] = _tok(env, "d_kvw")
        _sec(env, "sync", _st)

    def _ldk(s):
        _wait(env, s, "sync", env.tok_cc["misc"])
        _wait(env, s, "sync", st["w2_free"])
        segs = _tile_segs(cfg.MISC_WK, cfg.MISC_WK + 64, cfg.MT_PC)
        ofs = 0
        for (r, t0, n) in segs:
            _bump(env, s.dma_start(
                out=env.w2buf[:, 128 * ofs:128 * (ofs + n)],
                in_=env.wt_misc[128 * r:128 * r + 128,
                                128 * t0:128 * (t0 + n)]), "d_w2A", 16)
            ofs += n
    _sec(env, "sync", _ldk)
    t_wk = _tok(env, "d_w2A")

    ktoks = {}
    kguard = {0: None, 1: None}
    kpend = [None]

    def _kflush():
        if kpend[0] is not None:
            pbk, pmm, pout, ppar = kpend[0]
            kpend[0] = None
            kguard[ppar] = _emit_proj_tail(env, pbk, pmm, pout,
                                           env.ks2[:, 0:1], ktoks)

    for kvh in range(NKV):
        for co in range(2):
            par = (2 * kvh + co) % 2
            bk = env.ps[2 + par]
            t_mm = _emit_proj_mm(env, bk, t_wk,
                                 env.w2buf[:, 2048 * kvh:2048 * kvh + 2048],
                                 co, bank_guard=kguard[par])
            _kflush()
            kpend[0] = (bk, t_mm,
                        env.ktstage[:, TPC * kvh + 512 * co:
                                    TPC * kvh + 512 * co + 512], par)
    _kflush()
    t_kst = _tok(env, "s_dve")
    tok_kproj_pe = _tok(env, "s_pe")

    def _stk(s):
        _wait(env, s, "sync", t_kst)
        for kvh in range(NKV):
            _bump(env, s.dma_start(
                out=env.kvk_in[128 * kvh:128 * kvh + 128, :],
                in_=env.ktstage[:, TPC * kvh:TPC * kvh + TPC]), "d_kvw", 16)
    _sec(env, "sync", _stk)
    _ = tok_kproj_pe


def _emit_attn_core(env):
    cfg = env.cfg
    st = env.att_state
    qtoks = {}
    qpend = [None]

    def _qflush():
        if qpend[0] is not None:
            pbk, pmm, pout, ppar, pwar = qpend[0]
            qpend[0] = None
            st["qguard"][ppar] = _emit_proj_tail(env, pbk, pmm, pout,
                                                 env.qs2[:, 0:1], qtoks,
                                                 extra_war=pwar)

    for q4 in range(4):
        heads = list(range(4 * q4, 4 * q4 + 4))
        for hi, h in enumerate(heads):
            wq = "d_wqA" if h % 2 == 0 else "d_wqB"

            def _ldq(s, h=h, wq=wq):
                _wait(env, s, "sync", env.tok_cc["misc"])
                _wait(env, s, "sync", st["wq_use"].get(h - 2))
                if h < 2:
                    _wait(env, s, "sync", _tok(env, "s_pe"))
                segs = _tile_segs(cfg.MISC_WQ + 16 * h, cfg.MISC_WQ + 16 * h + 16,
                                  cfg.MT_PC)
                ofs = 0
                for (r, t0, n) in segs:
                    _bump(env, s.dma_start(
                        out=env.w13buf[:, 4096 * (h % 2) + 128 * ofs:
                                       4096 * (h % 2) + 128 * (ofs + n)],
                        in_=env.wt_misc[128 * r:128 * r + 128,
                                        128 * t0:128 * (t0 + n)]), wq, 16)
                    ofs += n
                st["wq_load"][h] = _tok(env, wq)
            _sec(env, "sync", _ldq)
            for co in range(2):
                par = (2 * hi + co) % 2
                bk = env.ps[par]
                t_mm = _emit_proj_mm(
                    env, bk, st["wq_load"][h],
                    env.w13buf[:, 4096 * (h % 2):4096 * (h % 2) + 2048],
                    co, extra_pe=st["woadd_last"],
                    bank_guard=st["qguard"].get(par))
                _qflush()
                qpend[0] = (bk, t_mm,
                            env.qa[:, TPC * hi + 512 * co:
                                   TPC * hi + 512 * co + 512], par,
                            st["qa_free"].get((q4 - 1, hi)))
            st["wq_use"][h] = _tok(env, "s_pe")
        _qflush()
        tok_q = _tok(env, "s_dve")

        for ci in range(2):
            nunits = 4 if ci == 0 else 8
            tok_sq_last = _tok(env, "s_act")
            seed_pe = _tok(env, "s_pe")

            def _vz(v):
                _wait(env, v, "dve", st["srow_free"])
                _bump(env, v.memset(env.srow[0:1, :], 0.0), "s_dve")
            _sec(env, "vector", _vz)
            t_sz = _tok(env, "s_dve")

            pt0 = st["pt"]
            st["st_free"][pt0 - 1] = tok_sq_last
            st["st_free"][pt0 - 2] = tok_sq_last
            st["pt_free"][pt0 - 1] = seed_pe
            st["pt_free"][pt0 - 2] = seed_pe

            pend = [None]

            def _flush_pend():
                if pend[0] is not None:
                    t2f, v2f = pend[0]
                    pend[0] = None
                    _sec(env, "tensor", t2f)
                    if v2f is not None:
                        _sec(env, "vector", v2f)

            for u in range(nunits):
                gu = st["gu"]
                st["gu"] += 1
                ub = gu % 2
                r_, half = (u, 0) if u <= 3 else (7 - u, 1)
                ks = "d_kuA" if ub == 0 else "d_kuB"
                vs = "d_vuA" if ub == 0 else "d_vuB"
                ms = "d_m2A" if ub == 0 else "d_m2B"

                def _ldu(s, u=u, gu=gu, ub=ub, r_=r_, half=half, ci=ci,
                         ks=ks, vs=vs, kvh=q4):
                    _wait(env, s, "sync", env.tok_kv)
                    _wait(env, s, "sync", st["ku_use"].get(gu - 2))
                    _bump(env, s.dma_start(
                        out=env.kt_u[:, 2048 * ub + 512 * kvh:
                                     2048 * ub + 512 * kvh + 512],
                        in_=env.kvk_all[512 * r_ + 128 * kvh:
                                        512 * r_ + 128 * kvh + 128,
                                        512 * half:512 * half + 512]), ks, 16)
                    _bump(env, s.dma_start(
                        out=env.v_u[:, 2048 * ub + 512 * kvh:
                                    2048 * ub + 512 * kvh + 512].rearrange(
                                        "p (t c) -> p t c", t=4),
                        in_=env.kvv_all[
                            4096 * r_ + 1024 * kvh + 512 * half:
                            4096 * r_ + 1024 * kvh + 512 * half + 512,
                            :].rearrange("(t p) c -> p t c", p=128)), vs, 16)
                    st["ku_load"][gu] = _tok(env, ks)
                    st["vu_load"][gu] = _tok(env, vs)
                _sec(env, "sync", _ldu)

                def _ldm(g, u=u, gu=gu, ub=ub, ci=ci, ms=ms):
                    _wait(env, g, "pool", st["ku_use"].get(gu - 2))
                    _bump(env, g.dma_start(
                        out=env.m2b[:, 2048 * ub:2048 * ub + 2048].rearrange(
                            "p (k c) -> p k c", k=4),
                        in_=env.amask2[ci, u].rearrange("k p c -> p k c")), ms, 16)
                    st["m2_load"][gu] = _tok(env, ms)
                _sec(env, "gpsimd", _ldm)

                for hi, h in enumerate(heads):
                    kvh = h // (NH // NKV)
                    for kt in range(4):
                        pt = st["pt"]
                        st["pt"] += 1
                        pb2 = pt % 2
                        stbk = env.ps[pt % 2]

                        def _t(t, hi=hi, kvh=kvh, kt=kt, gu=gu, ub=ub, pt=pt,
                               stbk=stbk, ci=ci):
                            _wait(env, t, "pe", st["ku_load"][gu])
                            _wait(env, t, "pe", tok_q)
                            _wait(env, t, "pe", st["woadd_last"])
                            _wait(env, t, "pe", st["st_free"].get(pt - 2))
                            _bump(env, t.matmul(
                                stbk[:, :],
                                env.kt_u[:, 2048 * ub + 512 * kvh + 128 * kt:
                                         2048 * ub + 512 * kvh + 128 * kt + 128],
                                env.qa[:, TPC * hi + 512 * ci:
                                       TPC * hi + 512 * ci + 512],
                                start=True, stop=True), "s_pe")
                            st["st_mm"][pt] = _tok(env, "s_pe")
                        _sec(env, "tensor", _t)
                        _flush_pend()

                        def _s(s, hi=hi, pt=pt, pb2=pb2, stbk=stbk):
                            _wait(env, s, "act", st["st_mm"][pt])
                            _wait(env, s, "act", env.tok_cst)
                            _wait(env, s, "act", st["pt_free"].get(pt - 2))
                            _bump(env, s.activation(
                                env.ptb[:, 512 * pb2:512 * pb2 + 512],
                                stbk[:, :], AF.Exp,
                                bias=env.cst[:, 2:3]), "s_act")
                            st["exp_done"][pt] = _tok(env, "s_act")
                            st["st_free"][pt] = _tok(env, "s_act")
                        _sec(env, "scalar", _s)

                        def _v(v, pt=pt, pb2=pb2, kt=kt, gu=gu, ub=ub):
                            _wait(env, v, "dve", st["exp_done"][pt])
                            _wait(env, v, "dve", st["m2_load"][gu])
                            _bump(env, v.tensor_tensor(
                                env.ptb[:, 512 * pb2:512 * pb2 + 512],
                                env.ptb[:, 512 * pb2:512 * pb2 + 512],
                                env.m2b[:, 2048 * ub + 512 * kt:
                                        2048 * ub + 512 * kt + 512],
                                ALU.mult), "s_dve")
                            st["pt_mask"][pt] = _tok(env, "s_dve")
                        _sec(env, "vector", _v)

                        def _t2(t, hi=hi, kvh=kvh, kt=kt, gu=gu, ub=ub, pt=pt,
                                pb2=pb2, u=u, nunits=nunits):
                            _wait(env, t, "pe", st["pt_mask"][pt])
                            _wait(env, t, "pe", st["vu_load"][gu])
                            if u == 0 and kt == 0:
                                _wait(env, t, "pe", st["attnT"].get(hi))
                            sbk = env.ps[6 + hi % 2]
                            if kt == 0:
                                _wait(env, t, "pe",
                                      st["srow_ev"].get((hi - 2) % 4))
                                _wait(env, t, "pe", st["srow_ev"].get(hi))
                                if hi % 2 == 0:
                                    _wait(env, t, "pe", st["ps6v"])
                            t.matmul(sbk[0:1, :], env.onescb[:, :],
                                     env.ptb[:, 512 * pb2:512 * pb2 + 512],
                                     start=(kt == 0), stop=(kt == 3))
                            pvb = env.ps[2 + hi]
                            _bump(env, t.matmul(
                                pvb[:, :],
                                env.v_u[:, 2048 * ub + 512 * kvh + 128 * kt:
                                        2048 * ub + 512 * kvh + 128 * kt + 128],
                                env.ptb[:, 512 * pb2:512 * pb2 + 512],
                                start=(u == 0 and kt == 0),
                                stop=(u == nunits - 1 and kt == 3)), "s_pe")
                            st["pt_free"][pt] = _tok(env, "s_pe")
                            if kt == 3:
                                st["srow_mm"][(hi, u)] = _tok(env, "s_pe")
                            if u == nunits - 1 and kt == 3:
                                st["pv_done"][hi] = _tok(env, "s_pe")
                            if hi == 3 and kt == 3:
                                st["ku_use"][gu] = _tok(env, "s_pe")

                        if kt == 3:
                            def _v2(v, hi=hi, u=u):
                                sbk = env.ps[6 + hi % 2]
                                _wait(env, v, "dve", st["srow_mm"][(hi, u)])
                                if u == 0:
                                    _wait(env, v, "dve", t_sz)
                                sl = env.srow[0:1, 512 * hi:512 * hi + 512]
                                _wait(env, v, "dve", st["srow_ev_v"].get(hi))
                                _bump(env, v.tensor_tensor(sl, sl, sbk[0:1, :],
                                                           ALU.add), "s_dve")
                                st["srow_ev"][hi] = _tok(env, "s_dve")
                                st["srow_ev_v"][hi] = _tok(env, "s_dve")
                        else:
                            _v2 = None
                        pend[0] = (_t2, _v2)
            _flush_pend()

            for hi, h in enumerate(heads):
                def _v(v, hi=hi):
                    _wait(env, v, "dve", st["srow_ev_v"].get(hi))
                    _bump(env, v.reciprocal(
                        env.irow[0:1, :],
                        env.srow[0:1, 512 * hi:512 * hi + 512]), "s_dve")
                _sec(env, "vector", _v)
                t_ir = _tok(env, "s_dve")

                def _t(t, hi=hi):
                    _wait(env, t, "pe", t_ir)
                    _wait(env, t, "pe", st["ps6v"])
                    bk = env.ps[6]
                    _bump(env, t.matmul(bk[0:128, :], env.onesr32[:, :],
                                        env.irow[0:1, :], start=True,
                                        stop=True), "s_pe")
                _sec(env, "tensor", _t)
                t_ib = _tok(env, "s_pe")

                def _v2(v, hi=hi, ci=ci):
                    bk = env.ps[6]
                    _wait(env, v, "dve", t_ib)
                    _bump(env, v.tensor_copy(env.invb[:, :], bk[0:128, :]),
                          "s_dve")
                    st["ps6v"] = _tok(env, "s_dve")
                    _wait(env, v, "dve", _tok(env, "s_dve"))
                    _wait(env, v, "dve", st["pv_done"][hi])
                    pvb = env.ps[2 + hi]
                    _bump(env, v.tensor_tensor(
                        env.qa[:, TPC * hi + 512 * ci:TPC * hi + 512 * ci + 512],
                        pvb[:, :], env.invb[:, :], ALU.mult), "s_dve")
                    st["attnT"][hi] = _tok(env, "s_dve")
                _sec(env, "vector", _v2)
            t_attnT = _tok(env, "s_dve")
            st["srow_free"] = t_attnT
            st["attnT_last"] = t_attnT

            hh = q4 // 2
            for n in range(4):
                gw = st["gw"]
                st["gw"] += 1
                wb = gw % 2
                ws = "d_w2A" if wb == 0 else "d_w2B"

                def _ldo(s, n=n, hh=hh, gw=gw, wb=wb, ws=ws):
                    _wait(env, s, "sync", env.tok_cc["misc"])
                    _wait(env, s, "sync", st["wo_use"].get(gw - 2))
                    a = cfg.MISC_WO + 64 * n + 32 * hh
                    segs = _tile_segs(a, a + 32, cfg.MT_PC)
                    ofs = 0
                    for (r, t0, m) in segs:
                        _bump(env, s.dma_start(
                            out=env.w2buf[:, 5632 * wb + 128 * ofs:
                                          5632 * wb + 128 * (ofs + m)],
                            in_=env.wt_misc[128 * r:128 * r + 128,
                                            128 * t0:128 * (t0 + m)]), ws, 16)
                        ofs += m
                    st["wo_load"][gw] = _tok(env, ws)
                _sec(env, "sync", _ldo)

                for tl in range(4):
                    tt = 4 * ci + tl
                    gm = st["gmo"]
                    st["gmo"] += 1
                    bk = env.ps[gm % 2]

                    def _t(t, n=n, tl=tl, q4=q4, ci=ci, gw=gw, wb=wb, gm=gm,
                           bk=bk):
                        _wait(env, t, "pe", t_attnT)
                        _wait(env, t, "pe", st["wo_load"][gw])
                        _wait(env, t, "pe", st["woadd"].get(gm - 2))
                        mm = None
                        for hi in range(4):
                            hpos = 4 * (q4 % 2) + hi
                            mm = t.matmul(bk[:, :],
                                          env.qa[:, TPC * hi + 512 * ci + 128 * tl:
                                                 TPC * hi + 512 * ci + 128 * tl + 128],
                                          env.w2buf[:, 5632 * wb + 512 * hpos:
                                                    5632 * wb + 512 * hpos + 512],
                                          start=(hi == 0), stop=(hi == 3))
                        _bump(env, mm, "s_pe")
                        st["womm"][gm] = _tok(env, "s_pe")
                        if tl == 3:
                            st["wo_use"][gw] = _tok(env, "s_pe")
                    _sec(env, "tensor", _t)

                    def _v(v, n=n, tt=tt, gm=gm, bk=bk):
                        _wait(env, v, "dve", st["womm"][gm])
                        xsl = env.x_sb[:, D * tt + 512 * n:D * tt + 512 * n + 512]
                        _bump(env, v.tensor_tensor(xsl, xsl, bk[:, :], ALU.add),
                              "s_dve")
                        st["woadd"][gm] = _tok(env, "s_dve")
                    _sec(env, "vector", _v)
            st["woadd_last"] = st["woadd"][st["gmo"] - 1]
        for hi in range(4):
            st["qa_free"][(q4, hi)] = _tok(env, "s_pe")

    env.tok_x_ready = st["woadd"][st["gmo"] - 1]
    env.tok_xlt_free = _tok(env, "s_pe")
    env.tok_wq_free = _tok(env, "s_pe")
    fs = env.ffn_state
    fs["w2_use"][fs["gn"] - 1] = _tok(env, "s_pe")
    fs["w2_use"][fs["gn"] - 2] = _tok(env, "s_pe")
    fs["pb_use"][fs["gp"] - 1] = _tok(env, "s_pe")
    fs["pb_use"][fs["gp"] - 2] = _tok(env, "s_pe")
    env.barrier_pe = [_tok(env, "s_pe")]
    env.barrier_dve = [_tok(env, "s_dve")]
    env.barrier_act = [_tok(env, "s_act")]



def _build(cfg):
    nc = bass.Bass("TRN2", target_bir_lowering=False)
    env = Env()
    env.nc = nc
    env.cfg = cfg
    l1 = (cfg.stage == "l1")
    lA = (cfg.stage == "lA")

    x_shard = nc.dram_tensor("x_shard", [2, 512, D], F32, kind="ExternalInput")
    ident_in = nc.dram_tensor("ident", [128, 128], F32, kind="ExternalInput")
    onesc_in = nc.dram_tensor("ones_col", [128, 1], F32, kind="ExternalInput")
    onesr_in = nc.dram_tensor("ones_row", [1, 128], F32, kind="ExternalInput")
    fconst_in = nc.dram_tensor("fconst", [128, 3], F32, kind="ExternalInput")
    out_shard = nc.dram_tensor("out_shard", [2, 512, D], F32, kind="ExternalOutput")

    env.amask2 = None
    if lA:
        conv_src = nc.dram_tensor("conv_src", [cfg.P_A, 2, 64, D], F32,
                                  kind="ExternalInput")
        conv_bv = nc.dram_tensor("conv_bv", [cfg.P_A, 2, 16, 64, 64], F32,
                                 kind="ExternalInput")
        conv_idx = nc.dram_tensor("conv_idx", [128, cfg.P_A], U32,
                                  kind="ExternalInput")
        env.y_dram = nc.dram_tensor("y_dram", [TPC + 128, D], F32)
        T_ALL = cfg.NE * cfg.N_ET + cfg.N_MT
        assert T_ALL % NCORES == 0
        TPCV = T_ALL // NCORES
        cvt_src = nc.dram_tensor("conv_wsrc", [TPCV, 128, 128], F32,
                                 kind="ExternalInput")
        cvt_out = nc.dram_tensor("cvt_out", [128, TPCV * 128], BF16,
                                 kind="ExternalOutput")
    if l1:
        ks_in = nc.dram_tensor("k_scale", [128, 1], F32, kind="ExternalInput")
        env.kvk_in = nc.dram_tensor("kvk_out", [512, TPC], BF16,
                                    kind="ExternalOutput")
        env.kvv_in = nc.dram_tensor("kvv_out", [4096, 128], BF16,
                                    kind="ExternalOutput")
    if cfg.stage == "l2":
        amask2 = nc.dram_tensor("amask2", [2, 8, 4, 128, 512], BF16,
                                kind="ExternalInput")
        qs_in = nc.dram_tensor("q_scale", [128, 1], F32, kind="ExternalInput")
        env.kvk_all = nc.dram_tensor("kvk_all", [2048, TPC], BF16,
                                     kind="ExternalInput")
        env.kvv_all = nc.dram_tensor("kvv_all", [16384, 128], BF16,
                                     kind="ExternalInput")
        env.amask2 = amask2

    if not lA:
        gate_in = nc.dram_tensor("gate_src", [16, D], F32, kind="ExternalInput")
        env.wt_all = [nc.dram_tensor(f"wt_e{e}", [128, cfg.N_ET * 128], BF16,
                                     kind="ExternalInput")
                      for e in range(cfg.NE)]
        env.wt_misc = nc.dram_tensor("wt_misc", [128, cfg.N_MT * 128], BF16,
                                     kind="ExternalInput")

    with contextlib.ExitStack() as stk:
        env.stack = stk

        def sb(name, shape, dt):
            return stk.enter_context(nc.sbuf_tensor(name, shape, dt))

        env.x_sb = sb("x_sb", [128, NTT * D], F32)
        env.xLT = sb("xLT", [128, NKT * TPC], BF16)
        env.w13buf = sb("w13buf", [128, 8192], BF16)
        env.w2buf = sb("w2buf", [128, 2 * 5632], BF16)
        env.sqs = sb("sqs", [128, D], F32)
        env.xls = sb("xls", [128, D], BF16)
        env.identbf = sb("identbf", [128, 128], BF16)
        env.ident32 = sb("ident32", [128, 128], F32)
        env.onescb = sb("onescb", [128, 1], BF16)
        env.onesc32 = sb("onesc32", [128, 1], F32)
        env.onesr32 = sb("onesr32", [1, 128], F32)
        env.qs2 = sb("qs2", [128, 1], F32)
        env.ks2 = sb("ks2", [128, 1], F32)
        env.gateT = sb("gateT", [128, 256], BF16)
        env.idx_all = sb("idx_all", [128, cfg.P_A], U32)
        env.colw = sb("colw", [128, 40], F32)
        env.gwork = sb("gwork", [128, 48], F32)
        env.combine = sb("combine", [128, 128], F32)
        env.invb = sb("invb", [128, 512], F32)
        env.acol = sb("acol", [128, 24], F32)
        env.mbf = sb("mbf", [128, 1], BF16)
        env.cst = sb("cst", [128, 3], F32)

        AR = 28800
        arena = sb("arena", [128, AR], BF16)

        def carve(lo_part, hi_part, col0, ncols_bf16, dt):
            v = arena[lo_part:hi_part, col0:col0 + ncols_bf16]
            return v if dt == BF16 else v.bitcast(dt)

        env.gstage = carve(0, 16, 4224, 2048, BF16)
        env.craw = carve(0, 128, 4224, 8192, F32)
        env.cblk = carve(0, 128, 12416, 4096, BF16)
        env.bvstage = carve(0, 64, 16512, 4096, BF16)
        env.bvT = carve(0, 128, 20608, 2048, BF16)
        env.scat = carve(0, 128, 22656, 4096, F32)
        env.h_sb = carve(0, 128, 4224, 11264, BF16)
        env.cb = carve(0, 128, 15488, 4096, F32)
        env.silu_t = carve(0, 128, 19584, 2048, F32)
        env.combT = carve(0, 16, 21632, 2048, F32)
        env.qa = carve(0, 128, 4224, 4096, BF16)
        env.kt_u = carve(0, 128, 8320, 4096, BF16)
        env.v_u = carve(0, 128, 12416, 4096, BF16)
        env.ktstage = carve(0, 128, 12416, 4096, BF16)
        env.m2b = carve(0, 128, 16512, 4096, BF16)
        env.vstage = carve(0, 128, 16512, 512, BF16)
        env.ptb = carve(0, 128, 20608, 1024, BF16)
        env.srow = carve(0, 1, 21632, 4096, F32)
        env.irow = carve(0, 1, 25728, 1024, F32)
        env.nrow = carve(0, 1, 26752, 2048, F32)

        env.ps = [stk.enter_context(nc.psum_tensor(f"ps{i}", [128, 512], F32))
                  for i in range(8)]

        env.hdl, env.val, env.waited = {}, {}, {}
        _mk_sems(env, ["d_xs", "d_csrcA", "d_csrcB", "d_bvA", "d_bvB", "d_wt",
                       "d_woA", "d_woB", "d_woC", "d_wtA", "d_wtB", "d_wtC",
                       "d_pbA", "d_pbB", "d_w2A", "d_w2B", "d_y", "d_ylA", "d_ylB", "d_sc",
                       "d_cc", "d_kvw", "d_kuA", "d_kuB", "d_vuA", "d_vuB",
                       "d_m2A", "d_m2B", "d_out", "d_smp", "d_sms",
                       "d_wqA", "d_wqB",
                       "s_pe", "s_dve", "s_act"])
        env.block = stk.enter_context(nc.Block())
        env.rec = {e: Rec() for e in ("tensor", "vector", "scalar", "sync",
                                      "gpsimd")}
        env.barrier_pe = []
        env.barrier_dve = []
        env.barrier_act = []
        env.t_sqs_free = None

        def _c_pool(g):
            _bump(env, g.dma_start(out=env.identbf[:, :], in_=ident_in[:, :]),
                  "d_smp", 16)
            _bump(env, g.dma_start(out=env.onescb[:, :], in_=onesc_in[:, :]),
                  "d_smp", 16)
            if not lA:
                _bump(env, g.dma_start(out=env.gstage[:, :], in_=gate_in[:, :]),
                      "d_smp", 16)
        _sec(env, "gpsimd", _c_pool)
        tok_smp = _tok(env, "d_smp")
        env.tok_smallbf = tok_smp

        def _c_sync(s):
            _bump(env, s.dma_start(out=env.ident32[:, :], in_=ident_in[:, :]),
                  "d_sms", 16)
            _bump(env, s.dma_start(out=env.onesc32[:, :], in_=onesc_in[:, :]),
                  "d_sms", 16)
            _bump(env, s.dma_start(out=env.onesr32[:, :], in_=onesr_in[:, :]),
                  "d_sms", 16)
            if l1:
                _bump(env, s.dma_start(out=env.ks2[:, :], in_=ks_in[:, :]),
                      "d_sms", 16)
            if lA:
                _bump(env, s.dma_start(out=env.idx_all[:, :],
                                       in_=conv_idx[:, :]), "d_sms", 16)
            if cfg.stage == "l2":
                _bump(env, s.dma_start(out=env.qs2[:, :], in_=qs_in[:, :]),
                      "d_sms", 16)
            _bump(env, s.dma_start(out=env.cst[:, :], in_=fconst_in[:, :]),
                  "d_sms", 16)
            for tt in range(NTT):
                ci, r = divmod(tt, 4)
                _bump(env, s.dma_start(out=env.x_sb[:, D * tt:D * tt + D],
                                       in_=x_shard[ci, 128 * r:128 * r + 128, :]),
                      "d_xs", 16)
        _sec(env, "sync", _c_sync)
        tok_sms = _tok(env, "d_sms")
        env.tok_cst = tok_sms

        def _c_dve(v):
            _wait(env, v, "dve", tok_sms)
            if cfg.stage == "l2":
                _bump(env, v.tensor_scalar_mul(env.qs2[:, :], env.qs2[:, :],
                                               1.0 / math.sqrt(HD)), "s_dve")
            else:
                _bump(env, v.engine_nop(), "s_dve")
        _sec(env, "vector", _c_dve)
        env.tok_qs2 = _tok(env, "s_dve")

        gts = []
        for kt in range(NKT if not lA else 0):
            def _t(t, kt=kt):
                _wait(env, t, "pe", tok_smp)
                bk = env.ps[6 + kt % 2]
                _wait(env, t, "pe", gts[-2] if len(gts) >= 2 else None)
                _bump(env, t.transpose(bk[:, :].bitcast(BF16)[0:128, 0:16],
                                       env.gstage[:, 128 * kt:128 * kt + 128],
                                       env.identbf[0:16, 0:16]), "s_pe")
            _sec(env, "tensor", _t)
            t_tr = _tok(env, "s_pe")

            def _v(v, kt=kt):
                bk = env.ps[6 + kt % 2]
                _wait(env, v, "dve", t_tr)
                _bump(env, v.tensor_copy(env.gateT[:, 16 * kt:16 * kt + 16],
                                         bk[:, :].bitcast(BF16)[0:128, 0:16]),
                      "s_dve")
                gts.append(_tok(env, "s_dve"))
            _sec(env, "vector", _v)
        env.tok_gateT = _tok(env, "s_dve") if not lA else None

        env.tok_cc = {e: None for e in range(cfg.NE)}
        env.tok_cc["misc"] = None

        if lA:
            WSC = 16
            cslots = [carve(0, 128, 0, 2048, BF16),
                      carve(0, 128, 2048, 2048, BF16),
                      carve(0, 128, 26752, 2048, BF16)]
            wosem = ["d_woA", "d_woB", "d_woC"]
            wtsem = ["d_wtA", "d_wtB", "d_wtC"]
            nbatch = (TPCV + WSC - 1) // WSC
            cst_state = dict(bi=0, sgc=0, prev={}, evs=[])

            def cvt_step():
                bi = cst_state["bi"]
                if bi >= nbatch:
                    return
                cst_state["bi"] += 1
                t0 = WSC * bi
                cnt = min(WSC, TPCV - t0)
                slot = cslots[bi % 3]

                def _ld(g, bi=bi, t0=t0, cnt=cnt, slot=slot):
                    _wait(env, g, "pool", cst_state["prev"].get(bi - 3))
                    _bump(env, g.dma_start(
                        out=slot[:, 0:128 * cnt].rearrange(
                            "p (t c) -> p t c", t=cnt),
                        in_=cvt_src[t0:t0 + cnt].rearrange("t p c -> p t c")),
                        wtsem[bi % 3], 16)
                _sec(env, "gpsimd", _ld)
                t_ld = _tok(env, wtsem[bi % 3])

                for lo in range(0, cnt, 8):
                    hi = min(lo + 8, cnt)
                    sg = cst_state["sgc"]
                    cst_state["sgc"] += 1
                    bk = env.ps[6 + sg % 2]

                    def _t(t, lo=lo, hi=hi, sg=sg, bk=bk, t_ld=t_ld,
                           slot=slot):
                        _wait(env, t, "pe", tok_smp)
                        _wait(env, t, "pe", t_ld)
                        _wait(env, t, "pe",
                              cst_state["evs"][sg - 2] if sg >= 2 else None)
                        for jj in range(lo, hi):
                            _bump(env, t.transpose(
                                bk[:, :].bitcast(BF16)[:, 128 * (jj - lo):
                                                       128 * (jj - lo) + 128],
                                slot[:, 128 * jj:128 * jj + 128],
                                env.identbf[:, :]), "s_pe")
                    _sec(env, "tensor", _t)
                    t_tr = _tok(env, "s_pe")

                    def _v(v, lo=lo, hi=hi, bk=bk, t_tr=t_tr, slot=slot):
                        _wait(env, v, "dve", t_tr)
                        _bump(env, v.tensor_copy(
                            slot[:, 128 * lo:128 * hi],
                            bk[:, :].bitcast(BF16)[:, 0:128 * (hi - lo)]),
                            "s_dve")
                        cst_state["evs"].append(_tok(env, "s_dve"))
                    _sec(env, "vector", _v)
                t_ev = _tok(env, "s_dve")

                def _out(s, bi=bi, t0=t0, cnt=cnt, t_ev=t_ev, slot=slot):
                    _wait(env, s, "act", t_ev)
                    _bump(env, s.dma_start(
                        out=cvt_out[:, 128 * t0:128 * (t0 + cnt)],
                        in_=slot[:, 0:128 * cnt]), wosem[bi % 3], 16)
                _sec(env, "scalar", _out)
                cst_state["prev"][bi] = _tok(env, wosem[bi % 3])

            env.cvt_step = cvt_step
            env.cvt_nbatch = nbatch
        else:
            env.cvt_step = lambda: None
            env.cvt_nbatch = 0

        env.tok_x_ready = _tok(env, "d_xs")
        if cfg.do_conv:
            def _z(v):
                _bump(env, v.memset(env.scat[:, 0:512], 0.0), "s_dve")
            _sec(env, "vector", _z)
            t_z = _tok(env, "s_dve")

            def _zs(s):
                _wait(env, s, "sync", t_z)
                for rt in range(NTT):
                    for cq in range(4):
                        _bump(env, s.dma_start(
                            out=env.y_dram[128 * rt:128 * rt + 128,
                                           512 * cq:512 * cq + 512],
                            in_=env.scat[:, 0:512]), "d_y", 16)
            _sec(env, "sync", _zs)
            t_yz = _tok(env, "d_y")

            ln_done = {}
            bvt_pe = {}
            blk_use = {}
            bvT_use = {}
            mm_tok = {}
            sc_ev = {}
            scat_done = {}
            for sl in range(cfg.P_A):
                for _ in range(3):
                    env.cvt_step()
                buf = sl % 2
                cs = "d_csrcA" if buf == 0 else "d_csrcB"
                bs = "d_bvA" if buf == 0 else "d_bvB"

                def _ld(s, sl=sl, buf=buf, cs=cs):
                    _wait(env, s, "sync", ln_done.get(sl - 2))
                    if sl < 2:
                        _wait(env, s, "sync", env.tok_gateT)
                    _bump(env, s.dma_start(
                        out=env.craw[0:64, D * buf:D * buf + D],
                        in_=conv_src[sl, 0]), cs, 16)
                    _bump(env, s.dma_start(
                        out=env.craw[64:128, D * buf:D * buf + D],
                        in_=conv_src[sl, 1]), cs, 16)
                _sec(env, "sync", _ld)
                t_cs = _tok(env, cs)

                def _ldb(g, sl=sl, buf=buf, bs=bs):
                    _wait(env, g, "pool", bvt_pe.get(sl - 2))
                    _bump(env, g.dma_start(
                        out=env.bvstage[:, 2048 * buf:
                                        2048 * buf + 1024].rearrange(
                                            "p (k j) -> p k j", k=16),
                        in_=conv_bv[sl, 0].rearrange("k i j -> i k j")), bs, 16)
                    _bump(env, g.dma_start(
                        out=env.bvstage[:, 2048 * buf + 1024:
                                        2048 * buf + 2048].rearrange(
                                            "p (k j) -> p k j", k=16),
                        in_=conv_bv[sl, 1].rearrange("k i j -> i k j")), bs, 16)
                _sec(env, "gpsimd", _ldb)
                t_bs = _tok(env, bs)

                def _lnsq(s, sl=sl, buf=buf, t_cs=t_cs):
                    _wait(env, s, "act", t_cs)
                    _wait(env, s, "act", env.t_sqs_free)
                    _bump(env, s.activation(env.sqs[:, :],
                                            env.craw[:, D * buf:D * buf + D],
                                            AF.Square), "s_act")
                _sec(env, "scalar", _lnsq)
                t_sq0 = _tok(env, "s_act")

                def _ln(v, sl=sl, buf=buf, t_cs=t_cs, t_sq0=t_sq0):
                    _wait(env, v, "dve", t_cs)
                    _wait(env, v, "dve", blk_use.get(sl - 2))
                    cw = env.colw
                    xt = env.craw[:, D * buf:D * buf + D]
                    _bump(env, v.reduce_sum(cw[:, 24:25], xt, axis=AX.X), "s_dve")
                    _wait(env, v, "dve", t_sq0)
                    _bump(env, v.reduce_sum(cw[:, 25:26], env.sqs[:, :],
                                            axis=AX.X), "s_dve")
                    env.t_sqs_free = _tok(env, "s_dve")
                    _wait(env, v, "dve", _tok(env, "s_dve"))
                    _bump(env, v.tensor_tensor(cw[:, 26:27], cw[:, 24:25],
                                               cw[:, 24:25], ALU.mult), "s_dve")
                    _wait(env, v, "dve", _tok(env, "s_dve"))
                    _bump(env, v.scalar_tensor_tensor(cw[:, 27:28], cw[:, 26:27],
                                                      -1.0 / D, cw[:, 25:26],
                                                      ALU.mult, ALU.add),
                          "s_dve")
                _sec(env, "vector", _ln)
                t_v1 = _tok(env, "s_dve")

                def _lns(s, sl=sl):
                    _wait(env, s, "act", t_v1)
                    _wait(env, s, "act", env.tok_cst)
                    _bump(env, s.activation(env.colw[:, 28:29], env.colw[:, 27:28],
                                            AF.Sqrt, bias=env.cst[:, 0:1],
                                            scale=1.0 / D), "s_act")
                _sec(env, "scalar", _lns)
                t_sq = _tok(env, "s_act")

                def _ln2(v, sl=sl, buf=buf):
                    cw = env.colw
                    _wait(env, v, "dve", t_sq)
                    _bump(env, v.reciprocal(cw[:, 29:30], cw[:, 28:29]), "s_dve")
                    _wait(env, v, "dve", _tok(env, "s_dve"))
                    _bump(env, v.tensor_scalar_mul(cw[:, 30:31], cw[:, 24:25],
                                                   1.0 / D), "s_dve")
                    _wait(env, v, "dve", _tok(env, "s_dve"))
                    _bump(env, v.tensor_scalar(env.cblk[:, D * buf:D * buf + D],
                                               env.craw[:, D * buf:D * buf + D],
                                               cw[:, 30:31], cw[:, 29:30],
                                               ALU.subtract, ALU.mult), "s_dve")
                    ln_done[sl] = _tok(env, "s_dve")
                _sec(env, "vector", _ln2)
                t_blk = _tok(env, "s_dve")

                def _bt(t, sl=sl, buf=buf, t_bs=t_bs):
                    _wait(env, t, "pe", t_bs)
                    _wait(env, t, "pe", tok_smp)
                    _wait(env, t, "pe", bvT_use.get(sl - 2))
                    _wait(env, t, "pe", sc_ev.get(("bv", sl - 2)))
                    bkA, bkB = env.ps[4], env.ps[5]
                    for k in range(16):
                        _bump(env, t.transpose(
                            bkA[:, :].bitcast(BF16)[0:64, 64 * k:64 * k + 64],
                            env.bvstage[:, 2048 * buf + 64 * k:
                                        2048 * buf + 64 * k + 64],
                            env.identbf[0:64, 0:64]), "s_pe")
                        _bump(env, t.transpose(
                            bkB[:, :].bitcast(BF16)[64:128, 64 * k:64 * k + 64],
                            env.bvstage[:, 2048 * buf + 1024 + 64 * k:
                                        2048 * buf + 1024 + 64 * k + 64],
                            env.identbf[0:64, 0:64],
                            tile_position=(0, 64)), "s_pe")
                    bvt_pe[sl] = _tok(env, "s_pe")
                _sec(env, "tensor", _bt)
                t_bt = _tok(env, "s_pe")

                def _bev(v, sl=sl, buf=buf):
                    _wait(env, v, "dve", t_bt)
                    bkA, bkB = env.ps[4], env.ps[5]
                    _bump(env, v.tensor_copy(
                        env.bvT[0:64, 1024 * buf:1024 * buf + 1024],
                        bkA[:, :].bitcast(BF16)[0:64, 0:1024]), "s_dve")
                    _bump(env, v.tensor_copy(
                        env.bvT[64:128, 1024 * buf:1024 * buf + 1024],
                        bkB[:, :].bitcast(BF16)[64:128, 0:1024]), "s_dve")
                    sc_ev[("bv", sl)] = _tok(env, "s_dve")
                _sec(env, "vector", _bev)
                t_bvT = _tok(env, "s_dve")

                for chq in range(4):
                    cu = 4 * sl + chq
                    pA = env.ps[0 if chq % 2 == 0 else 2]
                    pB = env.ps[1 if chq % 2 == 0 else 3]

                    def _cv(t, sl=sl, buf=buf, chq=chq, cu=cu, pA=pA, pB=pB,
                            t_blk=t_blk, t_bvT=t_bvT):
                        _wait(env, t, "pe", t_blk)
                        _wait(env, t, "pe", t_bvT)
                        _wait(env, t, "pe", sc_ev.get(("sc", cu - 2)))
                        first = True
                        mm = None
                        for k in range(15, -1, -1):
                            c0 = 512 * chq + k - 15
                            if c0 < 0:
                                w = 512 + c0
                                po = -c0
                                c0 = 0
                            else:
                                w, po = 512, 0
                            lA = env.bvT[0:64, 1024 * buf + 64 * k:
                                         1024 * buf + 64 * k + 64]
                            lB = env.bvT[64:128, 1024 * buf + 64 * k:
                                         1024 * buf + 64 * k + 64]
                            t.ldweights(lA, tile_position=(0, 0))
                            mmA = t.matmul(pA[0:64, po:po + w], lA,
                                           env.cblk[0:64, D * buf + c0:
                                                    D * buf + c0 + w],
                                           start=first, stop=(k == 0),
                                           tile_position=(0, 0))
                            t.ldweights(lB, tile_position=(64, 64))
                            mm = t.matmul(pB[64:128, po:po + w], lB,
                                          env.cblk[64:128, D * buf + c0:
                                                   D * buf + c0 + w],
                                          start=first, stop=(k == 0),
                                          tile_position=(64, 64))
                            first = False
                        _bump(env, mmA, "s_pe")
                        _bump(env, mm, "s_pe")
                        mm_tok[cu] = _tok(env, "s_pe")
                        if chq == 3:
                            blk_use[sl] = _tok(env, "s_pe")
                            bvT_use[sl] = _tok(env, "s_pe")
                    _sec(env, "tensor", _cv)

                    def _ev(v, sl=sl, chq=chq, cu=cu, pA=pA, pB=pB):
                        _wait(env, v, "dve", mm_tok[cu])
                        _wait(env, v, "dve", t_yz)
                        _wait(env, v, "dve", scat_done.get(sl - 1))
                        _bump(env, v.tensor_copy(
                            env.scat[0:64, 512 * chq:512 * chq + 512],
                            pA[0:64, :]), "s_dve")
                        _bump(env, v.tensor_copy(
                            env.scat[64:128, 512 * chq:512 * chq + 512],
                            pB[64:128, :]), "s_dve")
                        sc_ev[("sc", cu)] = _tok(env, "s_dve")
                    _sec(env, "vector", _ev)
                    t_evs = _tok(env, "s_dve")

                    if chq == 3:
                        def _scat(g, sl=sl, t_evs=t_evs):
                            _wait(env, g, "gpsimd", t_evs)
                            _wait(env, g, "gpsimd", t_yz)
                            _wait(env, g, "gpsimd", tok_sms)
                            _wait(env, g, "gpsimd",
                                  ("d_sc", env.val["d_sc"]))
                            _bump(env, g.indirect_dma_start(
                                out=env.y_dram[:, :],
                                out_offset=bass.IndirectOffsetOnAxis(
                                    ap=env.idx_all[:, sl:sl + 1], axis=0),
                                in_=env.scat[:, :],
                                in_offset=None,
                                compute_op=ALU.add), "d_sc", 16)
                            scat_done[sl] = _tok(env, "d_sc")
                        _sec(env, "gpsimd", _scat)

            t_allscat = _tok(env, "d_sc")
            yadds = []
            for tt in range(NTT):
                env.cvt_step()
                buf = tt % 2

                ysem = "d_ylA" if buf == 0 else "d_ylB"

                def _ldy(s, tt=tt, buf=buf, ysem=ysem):
                    _wait(env, s, "sync", t_allscat)
                    _wait(env, s, "sync", ln_done.get(cfg.P_A - 1))
                    _wait(env, s, "sync", ln_done.get(cfg.P_A - 2))
                    _wait(env, s, "sync", yadds[-2] if len(yadds) >= 2 else None)
                    _bump(env, s.dma_start(
                        out=env.craw[:, D * buf:D * buf + D],
                        in_=env.y_dram[128 * tt:128 * tt + 128, :]), ysem, 16)
                _sec(env, "sync", _ldy)
                t_ld = _tok(env, ysem)

                def _add(v, tt=tt, buf=buf):
                    _wait(env, v, "dve", t_ld)
                    _wait(env, v, "dve", ("d_xs", env.val["d_xs"]))
                    xt = env.x_sb[:, D * tt:D * tt + D]
                    _bump(env, v.tensor_tensor(
                        xt, xt, env.craw[:, D * buf:D * buf + D], ALU.add),
                        "s_dve")
                    yadds.append(_tok(env, "s_dve"))
                _sec(env, "vector", _add)
            for _ in range(env.cvt_nbatch):
                env.cvt_step()
            env.tok_x_ready = _tok(env, "s_dve")
            env.barrier_pe = [_tok(env, "s_pe")]
            env.barrier_dve = [_tok(env, "s_dve"), ("d_sc", env.val["d_sc"]),
                              ("d_y", env.val["d_y"])]
        else:
            def _w(v):
                _wait(env, v, "dve", ("d_xs", env.val["d_xs"]))
                _bump(env, v.engine_nop(), "s_dve")
            _sec(env, "vector", _w)
            env.tok_x_ready = _tok(env, "s_dve")
            env.barrier_pe = [_tok(env, "s_pe")]
            env.barrier_dve = [_tok(env, "s_dve")]

        env.tok_xlt_free = None
        env.t_xls_free = None
        env.t_sqs_free = None
        env.tok_h_free = None
        env.tok_wq_free = None
        env.ffn_state = dict(gp=0, gu=0, gn=0, gm=0, pb_load={}, pb_use={},
                             unit_mm={}, silu_done={}, op1_done={},
                             w2_load={}, w2_use={}, mm2_done={}, xadd_done={})
        env.att_state = dict(gu=0, pt=0, gw=0, gmo=0, wq_load={}, wq_use={},
                             qa_free={}, ku_load={}, ku_use={}, vu_load={},
                             m2_load={}, p1mm={}, st_mm={}, st_free={},
                             exp_done={}, pt_mask={}, pt_free={}, srow_mm={},
                             srow_ev={}, srow_ev_v={}, srow_free=None,
                             pv_done={}, attnT={}, wo_load={}, wo_use={},
                             womm={}, woadd={}, woadd_last=None, ps6v=None, attnT_last=None, qguard={},
                             w2_free=None)

        if lA:
            t_final = env.tok_x_ready
        elif l1:
            _emit_ln(env)
            _emit_ffn(env, 0)
            t_final = env.tok_x_ready
            env.att_state["w2_free"] = env.ffn_state["w2_use"].get(
                env.ffn_state["gn"] - 1)
            _emit_ln(env)
            _emit_attn_kv(env)
        else:
            env.tok_kv = None
            _emit_ln(env)
            _emit_attn_core(env)
            _emit_ln(env)
            _emit_ffn(env, 1)
            t_final = env.tok_x_ready

        tok_ci0 = getattr(env, "tok_x_ci0", None) if not lA else None

        def _out(s):
            for wave in range(2):
                tok = tok_ci0 if (wave == 0 and tok_ci0 is not None)                     else t_final
                _wait(env, s, "sync", tok)
                for tt in range(4 * wave, 4 * wave + 4):
                    ci, r = divmod(tt, 4)
                    _bump(env, s.dma_start(
                        out=out_shard[ci, 128 * r:128 * r + 128, :],
                        in_=env.x_sb[:, D * tt:D * tt + D]), "d_out", 16)
        _sec(env, "sync", _out)

        def _fin(g):
            g.wait_ge(env.hdl["d_out"], env.val["d_out"])
            if l1:
                g.wait_ge(env.hdl["d_kvw"], env.val["d_kvw"])
            if lA:
                for sname in ("d_woA", "d_woB", "d_woC",
                              "d_wtA", "d_wtB", "d_wtC"):
                    if env.val[sname]:
                        g.wait_ge(env.hdl[sname], env.val[sname])
        _sec(env, "gpsimd", _fin)

        _replay_all(env)

    return nc



def kernel(L0_attn_a_x, L0_attn_a_block_values, L0_attn_a_cols, L0_attn_a_rows,
           L0_attn_b_Wq, L0_attn_b_Wk, L0_attn_b_Wv, L0_attn_b_Wo,
           L0_attn_b_q_scale, L0_attn_b_k_scale, L0_ffn_gate_w, L0_ffn_w13,
           L0_ffn_w2, L0_ffn_sw13, L0_ffn_sw2):
    x = np.asarray(L0_attn_a_x, np.float32)
    bv = np.asarray(L0_attn_a_block_values, np.float32)
    cols = np.asarray(L0_attn_a_cols).astype(np.int64)
    rows = np.asarray(L0_attn_a_rows).astype(np.int64)
    plan = _plan(rows)
    cfgA = Cfg(P_A=plan[2], stage="lA")
    cfg1 = Cfg(P_A=plan[2], stage="l1", do_conv=False)
    cfg2 = Cfg(P_A=plan[2], stage="l2", do_conv=False)
    staged = []
    for c in range(NCORES):
        staged.append(_stage_core(
            cfgA, c, plan, x, bv, cols, rows,
            np.asarray(L0_attn_b_Wq, np.float32),
            np.asarray(L0_attn_b_Wk, np.float32),
            np.asarray(L0_attn_b_Wv, np.float32),
            np.asarray(L0_attn_b_Wo, np.float32),
            L0_attn_b_q_scale, L0_attn_b_k_scale,
            np.asarray(L0_ffn_gate_w, np.float32),
            np.asarray(L0_ffn_w13, np.float32),
            np.asarray(L0_ffn_w2, np.float32),
            np.asarray(L0_ffn_sw13, np.float32),
            np.asarray(L0_ffn_sw2, np.float32)))

    w13 = np.asarray(L0_ffn_w13, np.float32)
    w2 = np.asarray(L0_ffn_w2, np.float32)
    named = {"sw13": np.asarray(L0_ffn_sw13, np.float32),
             "sw2": np.asarray(L0_ffn_sw2, np.float32),
             "Wq": np.asarray(L0_attn_b_Wq, np.float32),
             "Wk": np.asarray(L0_attn_b_Wk, np.float32),
             "Wv": np.asarray(L0_attn_b_Wv, np.float32),
             "Wo": np.asarray(L0_attn_b_Wo, np.float32)}
    tiles = _all_tiles(cfgA)
    T_PC = len(tiles) // NCORES
    packs = []
    for c in range(NCORES):
        pack = np.empty((T_PC, 128, 128), np.float32)
        for i, (dst, t) in enumerate(tiles[T_PC * c:T_PC * (c + 1)]):
            if dst == "misc":
                kind, r0, c0 = _misc_tile_src(cfgA, t)
                a = named[kind]
            else:
                kind, r0, c0 = _expert_tile_src(cfgA, t)
                a = w13[dst] if kind == "w13" else w2[dst]
            pack[i] = a[r0:r0 + 128, c0:c0 + 128]
        packs.append(pack)

    keysA = ["x_shard", "conv_src", "conv_bv", "conv_idx", "ident", "ones_col",
             "ones_row", "fconst"]
    ncA = _build(cfgA)
    inA = [{**{k: staged[c][k] for k in keysA}, "conv_wsrc": packs[c]}
           for c in range(NCORES)]
    resA = run_bass_kernel_spmd(ncA, inA, core_ids=list(range(NCORES)))
    x1 = [np.ascontiguousarray(np.asarray(resA.results[c]["out_shard"]))
          for c in range(NCORES)]
    big = np.concatenate([np.asarray(resA.results[c]["cvt_out"])
                          for c in range(NCORES)], axis=1)
    wt_in = {}
    for e in range(cfgA.NE):
        wt_in[f"wt_e{e}"] = np.ascontiguousarray(
            big[:, 128 * cfgA.N_ET * e:128 * cfgA.N_ET * (e + 1)])
    wt_in["wt_misc"] = np.ascontiguousarray(big[:, 128 * cfgA.N_ET * cfgA.NE:])

    keys1 = ["ident", "ones_col", "ones_row", "k_scale", "gate_src", "fconst"]
    keys2 = ["amask2", "ident", "ones_col", "ones_row", "q_scale", "gate_src",
             "fconst"]

    nc1 = _build(cfg1)
    in1 = [{**{k: staged[c][k] for k in keys1}, **wt_in, "x_shard": x1[c]}
           for c in range(NCORES)]
    res1 = run_bass_kernel_spmd(nc1, in1, core_ids=list(range(NCORES)))
    x2 = [np.ascontiguousarray(np.asarray(res1.results[c]["out_shard"]))
          for c in range(NCORES)]
    kvk = [np.asarray(res1.results[c]["kvk_out"]) for c in range(NCORES)]
    kvv = [np.asarray(res1.results[c]["kvv_out"]) for c in range(NCORES)]
    kvk_g = [np.ascontiguousarray(np.concatenate(kvk[4 * g:4 * g + 4], axis=0))
             for g in range(2)]
    kvv_g = [np.ascontiguousarray(np.concatenate(kvv[4 * g:4 * g + 4], axis=0))
             for g in range(2)]

    nc2 = _build(cfg2)
    in2 = []
    for c in range(NCORES):
        m = {k: staged[c][k] for k in keys2}
        m.update(wt_in)
        m["x_shard"] = x2[c]
        m["kvk_all"] = kvk_g[c // 4]
        m["kvv_all"] = kvv_g[c // 4]
        in2.append(m)
    res2 = run_bass_kernel_spmd(nc2, in2, core_ids=list(range(NCORES)))

    out = np.empty((B, T, D), np.float32)
    for c, r in enumerate(res2.results):
        b, j = divmod(c, 4)
        sh = np.asarray(r["out_shard"])
        out[b, 512 * j:512 * j + 512] = sh[0]
        out[b, 512 * (7 - j):512 * (7 - j) + 512] = sh[1]
    return out


# version 13 (final)
# speedup vs baseline: 1.0857x; 1.0857x over previous
import math
import contextlib
import numpy as np
import ml_dtypes

import concourse.bass as bass
import concourse.mybir as mybir
from concourse.bass_utils import run_bass_kernel_spmd

F32 = mybir.dt.float32
BF16 = mybir.dt.bfloat16
U32 = mybir.dt.uint32
AF = mybir.ActivationFunctionType
ALU = mybir.AluOpType
AX = mybir.AxisListType

NCORES = 8
B, T, D = 2, 4096, 2048
TPC, CH, NTT, NKT = 1024, 512, 8, 16
NH, NKV, HD = 16, 4, 128
NEG = -1.0e30


class Cfg:
    def __init__(self, **kw):
        self.NE = 16
        self.PE_PAIRS = 11
        self.PS_PAIRS = 22
        self.W2KT = 11
        self.TOPK = 6
        self.P_A = 21
        self.stage = "l1"
        self.do_conv = True
        self.do_ffn1 = True
        self.do_attn = True
        self.do_ffn2 = True
        for k, v in kw.items():
            setattr(self, k, v)
        self.N_ET = 32 * self.PE_PAIRS + 16 * self.W2KT
        assert self.N_ET % NCORES == 0
        self.ET_PC = self.N_ET // NCORES
        self.MISC_SW13 = 0
        self.MISC_SW2 = 32 * self.PS_PAIRS
        self.MISC_WQ = self.MISC_SW2 + 32 * self.W2KT
        self.MISC_WK = self.MISC_WQ + 256
        self.MISC_WV = self.MISC_WK + 64
        self.MISC_WO = self.MISC_WV + 64
        self.N_MT = self.MISC_WO + 256
        assert self.N_MT % NCORES == 0
        self.MT_PC = self.N_MT // NCORES


def _expert_tile_src(cfg, t):
    n13 = 32 * cfg.PE_PAIRS
    if t < n13:
        p, r = divmod(t, 32)
        kt, gu = divmod(r, 2)
        return ("w13", 128 * p + (0 if gu == 0 else 128 * cfg.PE_PAIRS), 128 * kt)
    t -= n13
    n, r = divmod(t, 4 * cfg.W2KT)
    kt, s = divmod(r, 4)
    return ("w2", 512 * n + 128 * s, 128 * kt)


def _misc_tile_src(cfg, t):
    if t < 32 * cfg.PS_PAIRS:
        p, r = divmod(t, 32)
        kt, gu = divmod(r, 2)
        return ("sw13", 128 * p + (0 if gu == 0 else 128 * cfg.PS_PAIRS), 128 * kt)
    t -= 32 * cfg.PS_PAIRS
    if t < 32 * cfg.W2KT:
        n, r = divmod(t, 8 * cfg.W2KT)
        kh, r2 = divmod(r, 4 * cfg.W2KT)
        kt, s = divmod(r2, 4)
        return ("sw2", 512 * n + 128 * s, 128 * (cfg.W2KT * kh + kt))
    t -= 32 * cfg.W2KT
    if t < 256:
        ht, kt = divmod(t, 16)
        return ("Wq", 128 * ht, 128 * kt)
    t -= 256
    if t < 64:
        kvh, kt = divmod(t, 16)
        return ("Wk", 128 * kvh, 128 * kt)
    t -= 64
    if t < 64:
        kt, s = divmod(t, 4)
        return ("Wv", 128 * s, 128 * kt)
    t -= 64
    n, r = divmod(t, 64)
    hh, r2 = divmod(r, 32)
    h, s = divmod(r2, 4)
    return ("Wo", 512 * n + 128 * s, 128 * (8 * hh + h))


def _tile_segs(a, b, per_rank):
    return [(0, a, b - a)]


def _all_tiles(cfg):
    tiles = []
    for e in range(cfg.NE):
        for t in range(cfg.N_ET):
            tiles.append((e, t))
    for t in range(cfg.N_MT):
        tiles.append(("misc", t))
    return tiles



def _plan(rows):
    per_core = []
    for c in range(NCORES):
        b, j = divmod(c, 4)
        blocks = set(range(8 * j, 8 * j + 8)) | set(range(8 * (7 - j), 8 * (7 - j) + 8))
        groups = [g for g in range(128) if int(rows[g]) in blocks]
        per_core.append((b, j, blocks, groups))
    pairs_pc = []
    for (b, j, blocks, groups) in per_core:
        buckets = {}
        for g in groups:
            buckets.setdefault(int(rows[g]), []).append(g)
        pairs = []
        while True:
            order = [r for r in sorted(buckets, key=lambda r: -len(buckets[r]))
                     if buckets[r]]
            if not order:
                break
            if len(order) == 1:
                pairs.append((buckets[order[0]].pop(), None))
            else:
                pairs.append((buckets[order[0]].pop(), buckets[order[1]].pop()))
        pairs_pc.append(pairs)
    P_A = max(1, max(len(p) for p in pairs_pc))
    return per_core, pairs_pc, P_A


def _local_row(j, rb):
    lc = 0 if rb // 8 == j else 1
    return 512 * lc + 64 * (rb % 8)


def _stage_core(cfg, c, plan, x, bv, cols, rows, Wq, Wk, Wv, Wo, q_scale,
                k_scale, gate_w, w13, w2, sw13, sw2):
    per_core, pairs_pc, P_A = plan
    b, j, blocks, groups = per_core[c]
    pairs = pairs_pc[c]
    d = {}
    d["x_shard"] = np.ascontiguousarray(np.stack(
        [x[b, 512 * j:512 * j + 512], x[b, 512 * (7 - j):512 * (7 - j) + 512]]))
    csrc = np.zeros((cfg.P_A, 2, 64, D), np.float32)
    cbv = np.zeros((cfg.P_A, 2, 16, 64, 64), np.float32)
    cidx = np.tile((TPC + np.arange(128, dtype=np.uint32))[:, None],
                   (1, cfg.P_A))
    for s, pr in enumerate(pairs):
        for half, g in enumerate(pr):
            if g is None:
                continue
            cb = int(cols[g])
            csrc[s, half] = x[b, 64 * cb:64 * cb + 64, :]
            cbv[s, half] = np.transpose(bv[g], (2, 0, 1))
            base = _local_row(j, int(rows[g]))
            cidx[64 * half:64 * half + 64, s] = base + np.arange(64, dtype=np.uint32)
    d["conv_src"] = csrc
    d["conv_bv"] = cbv
    d["conv_idx"] = cidx
    d["w13f"] = w13
    d["w2f"] = w2
    d["sw13f"] = sw13
    d["sw2f"] = sw2
    d["Wqf"] = Wq
    d["Wkf"] = Wk
    d["Wvf"] = Wv
    d["Wof"] = Wo
    pos = (j, 7 - j)
    m2 = np.zeros((2, 8, 4, 128, 512), np.float32)
    for ci in range(2):
        qglob = 512 * pos[ci] + np.arange(512)[None, :]
        for u in range(8):
            for kt in range(4):
                kglob = (512 * u + 128 * kt) + np.arange(128)[:, None]
                m2[ci, u, kt] = (kglob <= qglob).astype(np.float32)
    d["amask2"] = m2.astype(ml_dtypes.bfloat16)
    d["ident"] = np.eye(128, dtype=np.float32)
    d["ones_col"] = np.ones((128, 1), np.float32)
    d["ones_row"] = np.ones((1, 128), np.float32)
    d["q_scale"] = np.ascontiguousarray(np.asarray(q_scale, np.float32).reshape(128, 1))
    d["k_scale"] = np.ascontiguousarray(np.asarray(k_scale, np.float32).reshape(128, 1))
    d["gate_src"] = np.ascontiguousarray(gate_w)
    fc = np.zeros((128, 3), np.float32)
    fc[:, 0] = 1e-5
    fc[:, 1] = 1e-6
    fc[:, 2] = -30.0
    d["fconst"] = fc
    return d



class Env:
    pass


class RecInstr:
    __slots__ = ("incs",)

    def __init__(self):
        self.incs = []

    def then_inc(self, sem, n):
        self.incs.append((sem, n))
        return self


class Rec:

    def __init__(self):
        self.ops = []

    def __getattr__(self, name):
        def call(*a, **k):
            r = RecInstr()
            self.ops.append((name, a, k, r))
            return r
        return call


def _replay(env, engname):
    ops = env.rec[engname].ops
    CH = 6000
    for lo in range(0, len(ops), CH):
        chunk = ops[lo:lo + CH]

        def run(eng, chunk=chunk):
            for (name, a, k, r) in chunk:
                instr = getattr(eng, name)(*a, **k)
                for (sem, n) in r.incs:
                    instr = instr.then_inc(sem, n)
        getattr(env.block, engname)(run)


def _replay_all(env):
    for engname in ("tensor", "vector", "scalar", "sync", "gpsimd"):
        _replay(env, engname)


def _mk_sems(env, names):
    for n in names:
        env.hdl[n] = env.stack.enter_context(env.nc.semaphore(n))
        env.val[n] = 0


def _tok(env, name):
    return (name, env.val[name])


def _bump(env, instr, name, k=1):
    instr.then_inc(env.hdl[name], k)
    env.val[name] += k
    return (name, env.val[name])


def _wait(env, eng, engname, tok):
    if tok is None:
        return
    name, v = tok
    if v <= 0:
        return
    key = (engname, name)
    if env.waited.get(key, 0) >= v:
        return
    eng.wait_ge(env.hdl[name], v)
    env.waited[key] = v


def _wait_many(env, eng, engname, toks):
    for t in toks:
        _wait(env, eng, engname, t)


def _sec(env, engname, fn):
    fn(env.rec[engname])


def _phase_barrier_toks(env):
    return [_tok(env, "s_pe"), _tok(env, "s_dve"), _tok(env, "s_act"),
            _tok(env, "d_sc"), _tok(env, "d_y")]



def _emit_ln(env):
    tok_x_ready = env.tok_x_ready
    tok_xlt_free = env.tok_xlt_free
    ev_last = [None]
    for tt in range(NTT):
        def _v(v, tt=tt):
            _wait(env, v, "dve", tok_x_ready)
            xt = env.x_sb[:, D * tt:D * tt + D]
            cw = env.colw
            v.reduce_sum(cw[:, 0:1], xt, axis=AX.X)
            _bump(env, v.engine_nop(), "s_dve") if False else None
            env.val["s_dve"] += 0
            v.then_inc_placeholder = None
        _sec(env, "vector", lambda v: None)

        def _s0(s, tt=tt):
            _wait(env, s, "act", tok_x_ready)
            _wait(env, s, "act", env.t_sqs_free)
            _bump(env, s.activation(env.sqs[:, :],
                                    env.x_sb[:, D * tt:D * tt + D],
                                    AF.Square), "s_act")
        _sec(env, "scalar", _s0)
        t_sq0 = _tok(env, "s_act")

        def _v1(v, tt=tt):
            _wait(env, v, "dve", tok_x_ready)
            xt = env.x_sb[:, D * tt:D * tt + D]
            cw = env.colw
            _bump(env, v.reduce_sum(cw[:, 0:1], xt, axis=AX.X), "s_dve")
            _wait(env, v, "dve", t_sq0)
            _bump(env, v.reduce_sum(cw[:, 1:2], env.sqs[:, :], axis=AX.X),
                  "s_dve")
            env.t_sqs_free = _tok(env, "s_dve")
            _wait(env, v, "dve", _tok(env, "s_dve"))
            _bump(env, v.tensor_tensor(cw[:, 2:3], cw[:, 0:1], cw[:, 0:1],
                                       ALU.mult), "s_dve")
            _wait(env, v, "dve", _tok(env, "s_dve"))
            _bump(env, v.scalar_tensor_tensor(cw[:, 3:4], cw[:, 2:3], -1.0 / D,
                                              cw[:, 1:2], ALU.mult, ALU.add),
                  "s_dve")
        _sec(env, "vector", _v1)
        t_var = _tok(env, "s_dve")

        def _s(s, tt=tt):
            _wait(env, s, "act", t_var)
            _wait(env, s, "act", env.tok_cst)
            _bump(env, s.activation(env.colw[:, 4:5], env.colw[:, 3:4],
                                    AF.Sqrt, bias=env.cst[:, 0:1],
                                    scale=1.0 / D), "s_act")
        _sec(env, "scalar", _s)
        t_sqrt = _tok(env, "s_act")

        def _v2(v, tt=tt):
            _wait(env, v, "dve", t_sqrt)
            cw = env.colw
            _bump(env, v.reciprocal(cw[:, 5:6], cw[:, 4:5]), "s_dve")
            _wait(env, v, "dve", _tok(env, "s_dve"))
            _bump(env, v.tensor_scalar_mul(cw[:, 6:7], cw[:, 0:1], 1.0 / D),
                  "s_dve")
            _wait(env, v, "dve", _tok(env, "s_dve"))
            _wait(env, v, "dve", env.t_xls_free)
            _bump(env, v.tensor_scalar(env.xls[:, :],
                                       env.x_sb[:, D * tt:D * tt + D],
                                       cw[:, 6:7], cw[:, 5:6], ALU.subtract,
                                       ALU.mult), "s_dve")
        _sec(env, "vector", _v2)
        t_xls = _tok(env, "s_dve")

        def _t(t, tt=tt):
            _wait(env, t, "pe", t_xls)
            _wait(env, t, "pe", env.tok_smallbf)
            if tt == 0:
                _wait(env, t, "pe", tok_xlt_free)
                _wait_many(env, t, "pe", env.barrier_pe)
            _wait(env, t, "pe", ev_last[0])
            for kt in range(NKT):
                bk = env.ps[6 + (kt // 8)]
                _bump(env, t.transpose(
                    bk[:, :].bitcast(BF16)[:, 128 * (kt % 8):128 * (kt % 8) + 128],
                    env.xls[:, 128 * kt:128 * kt + 128],
                    env.identbf[:, :]), "s_pe")
        _sec(env, "tensor", _t)
        env.t_xls_free = _tok(env, "s_pe")
        t_tr = _tok(env, "s_pe")

        def _v3(v, tt=tt):
            _wait(env, v, "dve", t_tr)
            for half in range(2):
                bk = env.ps[6 + half]
                out = env.xLT[:, :].rearrange("p (k q) -> p k q", k=NKT)[
                    :, 8 * half:8 * half + 8, 128 * tt:128 * tt + 128]
                src = bk[:, :].bitcast(BF16)[:, 0:1024].rearrange(
                    "p (k q) -> p k q", k=8)
                _bump(env, v.tensor_copy(out, src), "s_dve")
            ev_last[0] = _tok(env, "s_dve")
        _sec(env, "vector", _v3)
    env.tok_xlt_ready = _tok(env, "s_dve")



def _emit_gating(env):
    cfg = env.cfg
    NEx = cfg.NE
    g_toks = []
    for tt in range(NTT):
        def _t(t, tt=tt):
            _wait(env, t, "pe", env.tok_xlt_ready)
            _wait(env, t, "pe", env.tok_gateT)
            bk = env.ps[6 + tt % 2]
            _wait(env, t, "pe", g_toks[-2] if len(g_toks) >= 2 else None)
            mm = None
            for kt in range(NKT):
                mm = t.matmul(bk[0:128, 0:NEx],
                              env.xLT[:, TPC * kt + 128 * tt:TPC * kt + 128 * tt + 128],
                              env.gateT[:, 16 * kt:16 * kt + NEx],
                              start=(kt == 0), stop=(kt == NKT - 1))
            _bump(env, mm, "s_pe")
        _sec(env, "tensor", _t)
        t_log = _tok(env, "s_pe")

        def _v(v, tt=tt):
            bk = env.ps[6 + tt % 2]
            cw = env.colw
            _wait(env, v, "dve", t_log)
            _bump(env, v.reduce_max(cw[:, 8:9], bk[0:128, 0:NEx], axis=AX.X,
                                    negate=True), "s_dve")
        _sec(env, "vector", _v)
        t_negm = _tok(env, "s_dve")

        def _s(s, tt=tt):
            bk = env.ps[6 + tt % 2]
            _wait(env, s, "act", t_negm)
            _bump(env, s.activation(env.gwork[:, 0:NEx], bk[0:128, 0:NEx],
                                    AF.Exp, bias=env.colw[:, 8:9]), "s_act")
        _sec(env, "scalar", _s)
        t_e = _tok(env, "s_act")

        def _v2(v, tt=tt):
            cw = env.colw
            gw = env.gwork
            _wait(env, v, "dve", t_e)
            wa, wb = gw[:, 16:16 + NEx], gw[:, 32:32 + NEx]
            _bump(env, v.tensor_copy(wa, gw[:, 0:NEx]), "s_dve")
            bufs = [wa, wb]
            k = min(cfg.TOPK, NEx)
            eqb = gw[:, 32 + 16:32 + 16 + NEx] if False else None
            for r in range(k):
                _wait(env, v, "dve", _tok(env, "s_dve"))
                _bump(env, v.reduce_max(cw[:, 9 + r:10 + r], bufs[r % 2],
                                        axis=AX.X), "s_dve")
                _wait(env, v, "dve", _tok(env, "s_dve"))
                _bump(env, v.tensor_single_scalar(cw[:, 17:17 + NEx],
                                                  bufs[r % 2],
                                                  cw[:, 9 + r:10 + r],
                                                  ALU.is_equal), "s_dve")
                _wait(env, v, "dve", _tok(env, "s_dve"))
                _bump(env, v.scalar_tensor_tensor(bufs[(r + 1) % 2],
                                                  cw[:, 17:17 + NEx], NEG,
                                                  bufs[r % 2], ALU.mult,
                                                  ALU.add), "s_dve")
            thr = cw[:, 9 + k - 1:10 + k - 1]
            _wait(env, v, "dve", _tok(env, "s_dve"))
            _bump(env, v.tensor_single_scalar(gw[:, 16:16 + NEx], gw[:, 0:NEx],
                                              thr, ALU.is_ge), "s_dve")
            _wait(env, v, "dve", _tok(env, "s_dve"))
            _bump(env, v.tensor_tensor(gw[:, 32:32 + NEx], gw[:, 0:NEx],
                                       gw[:, 16:16 + NEx], ALU.mult), "s_dve")
            _wait(env, v, "dve", _tok(env, "s_dve"))
            _bump(env, v.reduce_sum(cw[:, 15:16], gw[:, 32:32 + NEx],
                                    axis=AX.X), "s_dve")
            _wait(env, v, "dve", _tok(env, "s_dve"))
            _bump(env, v.reciprocal(cw[:, 16:17], cw[:, 15:16]), "s_dve")
            _wait(env, v, "dve", _tok(env, "s_dve"))
            _bump(env, v.tensor_scalar_mul(env.combine[:, 16 * tt:16 * tt + NEx],
                                           gw[:, 32:32 + NEx], cw[:, 16:17]),
                  "s_dve")
            g_toks.append(_tok(env, "s_dve"))
        _sec(env, "vector", _v2)
    env.tok_combT = _tok(env, "s_dve")


def _emit_cb(env, e):
    ex_toks = []
    for tt in range(NTT):
        def _t(t, e=e, tt=tt):
            _wait(env, t, "pe", env.tok_combT)
            _wait(env, t, "pe", env.cb_free.get(e % 2))
            _wait(env, t, "pe", env.cb_prev_v2)
            _wait(env, t, "pe", ex_toks[-2] if len(ex_toks) >= 2 else None)
            bk = env.ps[6 + tt % 2]
            _bump(env, t.matmul(bk[0:1, 0:128],
                                env.combine[:, 16 * tt + e:16 * tt + e + 1],
                                env.ident32[:, :], start=True, stop=True),
                  "s_pe")
        _sec(env, "tensor", _t)
        t_mm = _tok(env, "s_pe")

        def _v(v, tt=tt, t_mm=t_mm):
            _wait(env, v, "dve", t_mm)
            bk = env.ps[6 + tt % 2]
            _bump(env, v.tensor_copy(env.nrow[0:1, 128 * tt:128 * tt + 128],
                                     bk[0:1, 0:128]), "s_dve")
            ex_toks.append(_tok(env, "s_dve"))
        _sec(env, "vector", _v)
    t_row = _tok(env, "s_dve")

    def _t2(t, e=e):
        _wait(env, t, "pe", t_row)
        for ch in range(2):
            bk = env.ps[6 + ch]
            _bump(env, t.matmul(bk[0:128, 0:512], env.onesr32[:, :],
                                env.nrow[0:1, 512 * ch:512 * ch + 512],
                                start=True, stop=True), "s_pe")
    _sec(env, "tensor", _t2)
    t_mm2 = _tok(env, "s_pe")

    def _v2(v, e=e):
        _wait(env, v, "dve", t_mm2)
        for ch in range(2):
            bk = env.ps[6 + ch]
            _bump(env, v.tensor_copy(
                env.cb[:, TPC * (e % 2) + 512 * ch:TPC * (e % 2) + 512 * ch + 512],
                bk[0:128, 0:512]), "s_dve")
    _sec(env, "vector", _v2)
    env.cb_ready[e] = _tok(env, "s_dve")
    env.cb_prev_v2 = _tok(env, "s_dve")



def _emit_ffn(env, call):
    cfg = env.cfg
    st = env.ffn_state
    _emit_gating(env)
    env.cb_ready = {}
    env.cb_free = {}
    env.cb_prev_v2 = None
    _emit_cb(env, 0)
    if cfg.NE > 1:
        _emit_cb(env, 1)
    barrier_dve = list(env.barrier_dve)
    barrier_pe = list(env.barrier_pe)

    env.tok_x_ci0 = None
    for e in range(cfg.NE + 1):
        shared = (e == cfg.NE)
        npairs = cfg.PS_PAIRS if shared else cfg.PE_PAIRS
        chunk_passes = [[0], [1]] if shared else [[0, 1]]
        for pass_chunks in chunk_passes:
            if shared and pass_chunks == [1]:
                env.tok_x_ci0 = env.ffn_state["xadd_done"].get(
                    env.ffn_state["gm"] - 1)
            for p in range(npairs):
                gp = st["gp"]
                st["gp"] += 1
                buf = gp % 2
                ldsem = "d_pbA" if buf == 0 else "d_pbB"

                def _ld(s, e=e, p=p, gp=gp, buf=buf, ldsem=ldsem, shared=shared):
                    _wait(env, s, "pool", env.tok_cc["misc" if shared else e])
                    _wait(env, s, "pool", st["pb_use"].get(gp - 2))
                    if shared:
                        segs = _tile_segs(32 * p, 32 * p + 32, cfg.MT_PC)
                        src = env.wt_misc
                    else:
                        segs = _tile_segs(32 * p, 32 * p + 32, cfg.ET_PC)
                        src = env.wt_all[e]
                    ofs = 0
                    for (r, t0, n) in segs:
                        _bump(env, s.dma_start(
                            out=env.w13buf[:, 4096 * buf + 128 * ofs:
                                           4096 * buf + 128 * (ofs + n)],
                            in_=src[128 * r:128 * r + 128,
                                    128 * t0:128 * (t0 + n)]), ldsem, 16)
                        ofs += n
                    st["pb_load"][gp] = _tok(env, ldsem)
                _sec(env, "gpsimd", _ld)

                for ch in pass_chunks:
                    gu = st["gu"]
                    st["gu"] += 1
                    par = gu % 2
                    psG, psU = env.ps[2 * par], env.ps[2 * par + 1]

                    def _t(t, e=e, p=p, ch=ch, gp=gp, gu=gu, buf=buf, par=par,
                           psG=psG, psU=psU, shared=shared,
                           first=(p == 0 and ch == pass_chunks[0])):
                        if first:
                            _wait(env, t, "pe", env.tok_cc["misc" if shared else e])
                            _wait(env, t, "pe", env.tok_xlt_ready)
                            _wait_many(env, t, "pe", barrier_pe)
                        _wait(env, t, "pe", st["pb_load"][gp])
                        _wait(env, t, "pe", st["silu_done"].get(gu - 2))
                        _wait(env, t, "pe", st["op1_done"].get(gu - 2))
                        mmG = mmU = None
                        for kt in range(NKT):
                            rhs = env.xLT[:, TPC * kt + 512 * ch:
                                          TPC * kt + 512 * ch + 512]
                            mmG = t.matmul(psG[:, :],
                                           env.w13buf[:, 4096 * buf + 256 * kt:
                                                      4096 * buf + 256 * kt + 128],
                                           rhs, start=(kt == 0),
                                           stop=(kt == NKT - 1))
                            mmU = t.matmul(psU[:, :],
                                           env.w13buf[:, 4096 * buf + 256 * kt + 128:
                                                      4096 * buf + 256 * kt + 256],
                                           rhs, start=(kt == 0),
                                           stop=(kt == NKT - 1))
                        _bump(env, mmG, "s_pe")
                        _bump(env, mmU, "s_pe")
                        st["unit_mm"][gu] = _tok(env, "s_pe")
                        if ch == pass_chunks[-1]:
                            st["pb_use"][gp] = _tok(env, "s_pe")
                    _sec(env, "tensor", _t)

                    def _s(s, gu=gu, par=par, psG=psG):
                        _wait(env, s, "act", st["unit_mm"][gu])
                        _wait(env, s, "act", st["op1_done"].get(gu - 2))
                        _bump(env, s.activation(
                            env.silu_t[:, 512 * par:512 * par + 512],
                            psG[:, :], AF.Silu), "s_act")
                        st["silu_done"][gu] = _tok(env, "s_act")
                    _sec(env, "scalar", _s)

                    def _v(v, e=e, p=p, ch=ch, gu=gu, par=par, psU=psU,
                           shared=shared):
                        _wait(env, v, "dve", st["silu_done"][gu])
                        _wait(env, v, "dve", st["unit_mm"][gu])
                        _wait(env, v, "dve", env.tok_h_free)
                        _wait_many(env, v, "dve", barrier_dve)
                        if shared:
                            _bump(env, v.tensor_tensor(
                                env.h_sb[:, 512 * p:512 * p + 512],
                                env.silu_t[:, 512 * par:512 * par + 512],
                                psU[:, :], ALU.mult), "s_dve")
                        else:
                            _bump(env, v.tensor_tensor(
                                env.silu_t[:, 512 * par:512 * par + 512],
                                env.silu_t[:, 512 * par:512 * par + 512],
                                psU[:, :], ALU.mult), "s_dve")
                            _wait(env, v, "dve", _tok(env, "s_dve"))
                            _wait(env, v, "dve", env.cb_ready[e])
                            _bump(env, v.tensor_tensor(
                                env.h_sb[:, TPC * p + 512 * ch:
                                         TPC * p + 512 * ch + 512],
                                env.silu_t[:, 512 * par:512 * par + 512],
                                env.cb[:, TPC * (e % 2) + 512 * ch:
                                       TPC * (e % 2) + 512 * ch + 512],
                                ALU.mult), "s_dve")
                        st["op1_done"][gu] = _tok(env, "s_dve")
                    _sec(env, "vector", _v)
            tok_h_ready = _tok(env, "s_dve")

            if (not shared) and (e + 2 <= cfg.NE - 1):
                env.cb_free[(e + 2) % 2] = st["op1_done"].get(st["gu"] - 1)
                _emit_cb(env, e + 2)

            w2kt = cfg.W2KT
            nch_list = ([(n, kh) for n in range(4) for kh in range(2)]
                        if shared else [(n, None) for n in range(4)])
            gns = {}

            def emit_w2_load(li):
                n, kh = nch_list[li]
                gn = st["gn"]
                st["gn"] += 1
                gns[(n, kh)] = gn
                buf = gn % 2
                ldsem = "d_w2A" if buf == 0 else "d_w2B"

                def _ld(s, e=e, n=n, kh=kh, gn=gn, buf=buf, ldsem=ldsem,
                        shared=shared):
                    _wait(env, s, "pool", env.tok_cc["misc" if shared else e])
                    _wait(env, s, "pool", st["w2_use"].get(gn - 2))
                    if shared:
                        a = cfg.MISC_SW2 + 2 * w2kt * 4 * n + w2kt * 4 * kh
                        segs = _tile_segs(a, a + 4 * w2kt, cfg.MT_PC)
                        src = env.wt_misc
                    else:
                        a = 32 * cfg.PE_PAIRS + 4 * w2kt * n
                        segs = _tile_segs(a, a + 4 * w2kt, cfg.ET_PC)
                        src = env.wt_all[e]
                    ofs = 0
                    for (r, t0, m) in segs:
                        _bump(env, s.dma_start(
                            out=env.w2buf[:, 5632 * buf + 128 * ofs:
                                          5632 * buf + 128 * (ofs + m)],
                            in_=src[128 * r:128 * r + 128,
                                    128 * t0:128 * (t0 + m)]), ldsem, 16)
                        ofs += m
                    st["w2_load"][gn] = _tok(env, ldsem)
                _sec(env, "gpsimd", _ld)

            emit_w2_load(0)
            if len(nch_list) > 1:
                emit_w2_load(1)

            if shared:
                for n in range(4):
                    gn0, gn1 = gns[(n, 0)], gns[(n, 1)]
                    for tl in range(4):
                        tt = 4 * pass_chunks[0] + tl
                        gm = st["gm"]
                        st["gm"] += 1
                        bk = env.ps[4 + gm % 2]

                        def _t(t, n=n, tl=tl, tt=tt, gm=gm, bk=bk, gn0=gn0,
                               gn1=gn1):
                            _wait(env, t, "pe", tok_h_ready)
                            _wait(env, t, "pe", st["w2_load"][gn0])
                            _wait(env, t, "pe", st["w2_load"][gn1])
                            _wait(env, t, "pe", st["xadd_done"].get(gm - 2))
                            mm = None
                            for kh in range(2):
                                wb = (gn0 if kh == 0 else gn1) % 2
                                for kt in range(w2kt):
                                    mm = t.matmul(
                                        bk[:, :],
                                        env.h_sb[:, 512 * (w2kt * kh + kt) +
                                                 128 * tl:
                                                 512 * (w2kt * kh + kt) +
                                                 128 * tl + 128],
                                        env.w2buf[:, 5632 * wb + 512 * kt:
                                                  5632 * wb + 512 * kt + 512],
                                        start=(kh == 0 and kt == 0),
                                        stop=(kh == 1 and kt == w2kt - 1))
                            _bump(env, mm, "s_pe")
                            st["mm2_done"][gm] = _tok(env, "s_pe")
                            st["w2_use"][gn0] = _tok(env, "s_pe")
                            st["w2_use"][gn1] = _tok(env, "s_pe")
                        _sec(env, "tensor", _t)

                        def _v(v, n=n, tt=tt, gm=gm, bk=bk):
                            _wait(env, v, "dve", st["mm2_done"][gm])
                            xsl = env.x_sb[:, D * tt + 512 * n:
                                           D * tt + 512 * n + 512]
                            _bump(env, v.tensor_tensor(xsl, xsl, bk[:, :],
                                                       ALU.add), "s_dve")
                            st["xadd_done"][gm] = _tok(env, "s_dve")
                        _sec(env, "vector", _v)
                    for li in (2 * n + 2, 2 * n + 3):
                        if li < len(nch_list):
                            emit_w2_load(li)
            else:
                for n in range(4):
                    gn = gns[(n, None)]
                    for tt in range(NTT):
                        gm = st["gm"]
                        st["gm"] += 1
                        bk = env.ps[4 + gm % 2]

                        def _t(t, e=e, n=n, tt=tt, gm=gm, gn=gn, bk=bk):
                            _wait(env, t, "pe", tok_h_ready)
                            _wait(env, t, "pe", st["w2_load"][gn])
                            _wait(env, t, "pe", st["xadd_done"].get(gm - 2))
                            mm = None
                            for kt in range(w2kt):
                                mm = t.matmul(
                                    bk[:, :],
                                    env.h_sb[:, TPC * kt + 128 * tt:
                                             TPC * kt + 128 * tt + 128],
                                    env.w2buf[:, 5632 * (gn % 2) + 512 * kt:
                                              5632 * (gn % 2) + 512 * kt + 512],
                                    start=(kt == 0), stop=(kt == w2kt - 1))
                            _bump(env, mm, "s_pe")
                            st["mm2_done"][gm] = _tok(env, "s_pe")
                            if tt == NTT - 1:
                                st["w2_use"][gn] = _tok(env, "s_pe")
                        _sec(env, "tensor", _t)

                        def _v(v, n=n, tt=tt, gm=gm, bk=bk):
                            _wait(env, v, "dve", st["mm2_done"][gm])
                            xsl = env.x_sb[:, D * tt + 512 * n:
                                           D * tt + 512 * n + 512]
                            _bump(env, v.tensor_tensor(xsl, xsl, bk[:, :],
                                                       ALU.add), "s_dve")
                            st["xadd_done"][gm] = _tok(env, "s_dve")
                        _sec(env, "vector", _v)
                    if n + 2 < 4:
                        emit_w2_load(n + 2)
            env.tok_h_free = st["mm2_done"][st["gm"] - 1]
    env.tok_x_ready = st["xadd_done"][st["gm"] - 1]
    env.tok_xlt_free = st["mm2_done"][st["gm"] - 1]
    env.tok_wq_free = st["pb_use"][st["gp"] - 1]
    env.barrier_pe = [_tok(env, "s_pe")]
    env.barrier_dve = [_tok(env, "s_dve")]
    env.barrier_act = [_tok(env, "s_act")]



def _emit_proj_mm(env, bk, tok_w, wbuf_ap, co, extra_pe=None, bank_guard=None):
    def _t(t):
        _wait(env, t, "pe", tok_w)
        _wait(env, t, "pe", env.tok_xlt_ready)
        _wait(env, t, "pe", bank_guard)
        _wait(env, t, "pe", extra_pe)
        mm = None
        for kt in range(NKT):
            mm = t.matmul(bk[:, :], wbuf_ap[:, 128 * kt:128 * kt + 128],
                          env.xLT[:, TPC * kt + 512 * co:TPC * kt + 512 * co + 512],
                          start=(kt == 0), stop=(kt == NKT - 1))
        _bump(env, mm, "s_pe")
    _sec(env, "tensor", _t)
    return _tok(env, "s_pe")


def _emit_proj_tail(env, bk, t_mm, out_ap, scale_col, toks, extra_war=None):
    def _s(s):
        _wait(env, s, "act", t_mm)
        _wait(env, s, "act", toks.get(("sqsfree",)))
        _wait(env, s, "act", env.t_sqs_free)
        _bump(env, s.activation(env.sqs[:, 0:512], bk[:, :], AF.Square), "s_act")
    _sec(env, "scalar", _s)
    t_sq = _tok(env, "s_act")

    def _t2(t):
        _wait(env, t, "pe", t_sq)
        _wait(env, t, "pe", toks.get(("ps7free",)))
        sb7 = env.ps[7]
        _bump(env, t.matmul(sb7[0:1, :], env.onesc32[:, :], env.sqs[:, 0:512],
                            start=True, stop=True), "s_pe")
    _sec(env, "tensor", _t2)
    t_ss = _tok(env, "s_pe")
    toks[("sqsfree",)] = t_ss
    env.t_sqs_free = t_ss

    def _s2(s):
        sb7 = env.ps[7]
        _wait(env, s, "act", t_ss)
        _wait(env, s, "act", env.tok_cst)
        _wait(env, s, "act", toks.get(("nrowfree",)))
        _bump(env, s.activation(env.nrow[0:1, 0:512], sb7[0:1, :], AF.Sqrt,
                                bias=env.cst[0:1, 1:2], scale=1.0 / HD), "s_act")
    _sec(env, "scalar", _s2)
    t_rs0 = _tok(env, "s_act")
    toks[("ps7free",)] = t_rs0

    def _v15(v):
        _wait(env, v, "dve", t_rs0)
        _bump(env, v.reciprocal(env.nrow[0:1, 512:1024], env.nrow[0:1, 0:512]),
              "s_dve")
    _sec(env, "vector", _v15)
    t_rs = _tok(env, "s_dve")
    toks[("nrowfree",)] = t_rs

    def _t3(t):
        sb6 = env.ps[6]
        _wait(env, t, "pe", t_rs)
        _wait(env, t, "pe", toks.get(("ps6free",)))
        _bump(env, t.matmul(sb6[0:128, :], env.onesr32[:, :],
                            env.nrow[0:1, 512:1024], start=True, stop=True),
              "s_pe")
    _sec(env, "tensor", _t3)
    t_bc = _tok(env, "s_pe")

    def _v(v):
        sb6 = env.ps[6]
        _wait(env, v, "dve", t_bc)
        _wait(env, v, "dve", toks.get(("invbfree",)))
        _bump(env, v.tensor_copy(env.invb[:, :], sb6[0:128, :]), "s_dve")
        toks[("ps6free",)] = _tok(env, "s_dve")
        _wait(env, v, "dve", _tok(env, "s_dve"))
        _wait(env, v, "dve", extra_war)
        _bump(env, v.scalar_tensor_tensor(out_ap, bk[:, :], scale_col,
                                          env.invb[:, :], ALU.mult, ALU.mult),
              "s_dve")
        toks[("invbfree",)] = _tok(env, "s_dve")
    _sec(env, "vector", _v)
    return _tok(env, "s_dve")



def _emit_attn_kv(env):
    cfg = env.cfg
    st = env.att_state
    barrier_pe = list(env.barrier_pe)
    barrier_dve = list(env.barrier_dve)

    def _ldv(s):
        _wait(env, s, "pool", env.tok_cc["misc"])
        _wait(env, s, "pool", env.tok_wq_free)
        _wait_many(env, s, "pool", barrier_dve)
        segs = _tile_segs(cfg.MISC_WV, cfg.MISC_WV + 64, cfg.MT_PC)
        ofs = 0
        for (r, t0, n) in segs:
            _bump(env, s.dma_start(
                out=env.w13buf[:, 128 * ofs:128 * (ofs + n)],
                in_=env.wt_misc[128 * r:128 * r + 128,
                                128 * t0:128 * (t0 + n)]), "d_wqA", 16)
            ofs += n
    _sec(env, "gpsimd", _ldv)
    t_wv = _tok(env, "d_wqA")

    kvw_last = [None]
    vst_free = {}
    for tt in range(NTT):
        bk = env.ps[tt % 2]

        def _t(t, tt=tt, bk=bk):
            _wait(env, t, "pe", env.tok_cc["misc"])
            _wait(env, t, "pe", t_wv)
            _wait(env, t, "pe", env.tok_xlt_ready)
            _wait_many(env, t, "pe", barrier_pe)
            _wait(env, t, "pe", vst_free.get(tt - 2))
            mm = None
            for kt in range(NKT):
                mm = t.matmul(bk[:, :],
                              env.xLT[:, TPC * kt + 128 * tt:
                                      TPC * kt + 128 * tt + 128],
                              env.w13buf[:, 512 * kt:512 * kt + 512],
                              start=(kt == 0), stop=(kt == NKT - 1))
            _bump(env, mm, "s_pe")
        _sec(env, "tensor", _t)
        t_vp = _tok(env, "s_pe")

        def _v(v, tt=tt, bk=bk):
            _wait(env, v, "dve", t_vp)
            _wait(env, v, "dve", kvw_last[0])
            _wait_many(env, v, "dve", barrier_dve)
            _bump(env, v.tensor_copy(env.vstage[:, :], bk[:, :]), "s_dve")
        _sec(env, "vector", _v)
        t_ev = _tok(env, "s_dve")
        vst_free[tt] = t_ev

        def _st(s, tt=tt):
            _wait(env, s, "sync", t_ev)
            for kvh in range(NKV):
                _bump(env, s.dma_start(
                    out=env.kvv_in[1024 * kvh + 128 * tt:
                                   1024 * kvh + 128 * tt + 128, :],
                    in_=env.vstage[:, 128 * kvh:128 * kvh + 128]), "d_kvw", 16)
            kvw_last[0] = _tok(env, "d_kvw")
        _sec(env, "sync", _st)

    def _ldk(s):
        _wait(env, s, "pool", env.tok_cc["misc"])
        _wait(env, s, "pool", st["w2_free"])
        segs = _tile_segs(cfg.MISC_WK, cfg.MISC_WK + 64, cfg.MT_PC)
        ofs = 0
        for (r, t0, n) in segs:
            _bump(env, s.dma_start(
                out=env.w2buf[:, 128 * ofs:128 * (ofs + n)],
                in_=env.wt_misc[128 * r:128 * r + 128,
                                128 * t0:128 * (t0 + n)]), "d_w2A", 16)
            ofs += n
    _sec(env, "gpsimd", _ldk)
    t_wk = _tok(env, "d_w2A")

    ktoks = {}
    kguard = {0: None, 1: None}
    kpend = [None]

    def _kflush():
        if kpend[0] is not None:
            pbk, pmm, pout, ppar = kpend[0]
            kpend[0] = None
            kguard[ppar] = _emit_proj_tail(env, pbk, pmm, pout,
                                           env.ks2[:, 0:1], ktoks)

    for kvh in range(NKV):
        for co in range(2):
            par = (2 * kvh + co) % 2
            bk = env.ps[2 + par]
            t_mm = _emit_proj_mm(env, bk, t_wk,
                                 env.w2buf[:, 2048 * kvh:2048 * kvh + 2048],
                                 co, bank_guard=kguard[par])
            _kflush()
            kpend[0] = (bk, t_mm,
                        env.ktstage[:, TPC * kvh + 512 * co:
                                    TPC * kvh + 512 * co + 512], par)
    _kflush()
    t_kst = _tok(env, "s_dve")
    tok_kproj_pe = _tok(env, "s_pe")

    def _stk(s):
        _wait(env, s, "sync", t_kst)
        for kvh in range(NKV):
            _bump(env, s.dma_start(
                out=env.kvk_in[128 * kvh:128 * kvh + 128, :],
                in_=env.ktstage[:, TPC * kvh:TPC * kvh + TPC]), "d_kvw", 16)
    _sec(env, "sync", _stk)
    _ = tok_kproj_pe


def _emit_attn_core(env):
    cfg = env.cfg
    st = env.att_state
    qtoks = {}
    qpend = [None]

    def _qflush():
        if qpend[0] is not None:
            pbk, pmm, pout, ppar, pwar = qpend[0]
            qpend[0] = None
            st["qguard"][ppar] = _emit_proj_tail(env, pbk, pmm, pout,
                                                 env.qs2[:, 0:1], qtoks,
                                                 extra_war=pwar)

    for q4 in range(4):
        heads = list(range(4 * q4, 4 * q4 + 4))
        for hi, h in enumerate(heads):
            wq = "d_wqA" if h % 2 == 0 else "d_wqB"

            def _ldq(s, h=h, wq=wq):
                _wait(env, s, "pool", env.tok_cc["misc"])
                _wait(env, s, "pool", st["wq_use"].get(h - 2))
                if h < 2:
                    _wait(env, s, "pool", _tok(env, "s_pe"))
                segs = _tile_segs(cfg.MISC_WQ + 16 * h, cfg.MISC_WQ + 16 * h + 16,
                                  cfg.MT_PC)
                ofs = 0
                for (r, t0, n) in segs:
                    _bump(env, s.dma_start(
                        out=env.w13buf[:, 4096 * (h % 2) + 128 * ofs:
                                       4096 * (h % 2) + 128 * (ofs + n)],
                        in_=env.wt_misc[128 * r:128 * r + 128,
                                        128 * t0:128 * (t0 + n)]), wq, 16)
                    ofs += n
                st["wq_load"][h] = _tok(env, wq)
            _sec(env, "gpsimd", _ldq)
            for co in range(2):
                par = (2 * hi + co) % 2
                bk = env.ps[par]
                t_mm = _emit_proj_mm(
                    env, bk, st["wq_load"][h],
                    env.w13buf[:, 4096 * (h % 2):4096 * (h % 2) + 2048],
                    co, extra_pe=st["woadd_last"],
                    bank_guard=st["qguard"].get(par))
                _qflush()
                qpend[0] = (bk, t_mm,
                            env.qa[:, TPC * hi + 512 * co:
                                   TPC * hi + 512 * co + 512], par,
                            st["qa_free"].get((q4 - 1, hi)))
            st["wq_use"][h] = _tok(env, "s_pe")
        _qflush()
        tok_q = _tok(env, "s_dve")

        for ci in range(2):
            nunits = 4 if ci == 0 else 8
            tok_sq_last = _tok(env, "s_act")
            seed_pe = _tok(env, "s_pe")

            def _vz(v):
                _wait(env, v, "dve", st["srow_free"])
                _bump(env, v.memset(env.srow[0:1, :], 0.0), "s_dve")
            _sec(env, "vector", _vz)
            t_sz = _tok(env, "s_dve")

            pt0 = st["pt"]
            st["st_free"][pt0 - 1] = tok_sq_last
            st["st_free"][pt0 - 2] = tok_sq_last
            st["pt_free"][pt0 - 1] = seed_pe
            st["pt_free"][pt0 - 2] = seed_pe

            pend = [None]

            def _flush_pend():
                if pend[0] is not None:
                    t2f, v2f = pend[0]
                    pend[0] = None
                    _sec(env, "tensor", t2f)
                    if v2f is not None:
                        _sec(env, "vector", v2f)

            for u in range(nunits):
                gu = st["gu"]
                st["gu"] += 1
                ub = gu % 2
                r_, half = (u, 0) if u <= 3 else (7 - u, 1)
                ks = "d_kuA" if ub == 0 else "d_kuB"
                vs = "d_vuA" if ub == 0 else "d_vuB"
                ms = "d_m2A" if ub == 0 else "d_m2B"

                def _ldu(s, u=u, gu=gu, ub=ub, r_=r_, half=half, ci=ci,
                         ks=ks, vs=vs, kvh=q4):
                    _wait(env, s, "sync", env.tok_kv)
                    _wait(env, s, "sync", st["ku_use"].get(gu - 2))
                    _bump(env, s.dma_start(
                        out=env.kt_u[:, 2048 * ub + 512 * kvh:
                                     2048 * ub + 512 * kvh + 512],
                        in_=env.kvk_all[512 * r_ + 128 * kvh:
                                        512 * r_ + 128 * kvh + 128,
                                        512 * half:512 * half + 512]), ks, 16)
                    _bump(env, s.dma_start(
                        out=env.v_u[:, 2048 * ub + 512 * kvh:
                                    2048 * ub + 512 * kvh + 512].rearrange(
                                        "p (t c) -> p t c", t=4),
                        in_=env.kvv_all[
                            4096 * r_ + 1024 * kvh + 512 * half:
                            4096 * r_ + 1024 * kvh + 512 * half + 512,
                            :].rearrange("(t p) c -> p t c", p=128)), vs, 16)
                    st["ku_load"][gu] = _tok(env, ks)
                    st["vu_load"][gu] = _tok(env, vs)
                _sec(env, "sync", _ldu)

                def _ldm(g, u=u, gu=gu, ub=ub, ci=ci, ms=ms):
                    _wait(env, g, "pool", st["ku_use"].get(gu - 2))
                    _bump(env, g.dma_start(
                        out=env.m2b[:, 2048 * ub:2048 * ub + 2048].rearrange(
                            "p (k c) -> p k c", k=4),
                        in_=env.amask2[ci, u].rearrange("k p c -> p k c")), ms, 16)
                    st["m2_load"][gu] = _tok(env, ms)
                _sec(env, "gpsimd", _ldm)

                for hi, h in enumerate(heads):
                    kvh = h // (NH // NKV)
                    for kt in range(4):
                        pt = st["pt"]
                        st["pt"] += 1
                        pb2 = pt % 2
                        stbk = env.ps[pt % 2]

                        def _t(t, hi=hi, kvh=kvh, kt=kt, gu=gu, ub=ub, pt=pt,
                               stbk=stbk, ci=ci):
                            _wait(env, t, "pe", st["ku_load"][gu])
                            _wait(env, t, "pe", tok_q)
                            _wait(env, t, "pe", st["woadd_last"])
                            _wait(env, t, "pe", st["st_free"].get(pt - 2))
                            _bump(env, t.matmul(
                                stbk[:, :],
                                env.kt_u[:, 2048 * ub + 512 * kvh + 128 * kt:
                                         2048 * ub + 512 * kvh + 128 * kt + 128],
                                env.qa[:, TPC * hi + 512 * ci:
                                       TPC * hi + 512 * ci + 512],
                                start=True, stop=True), "s_pe")
                            st["st_mm"][pt] = _tok(env, "s_pe")
                        _sec(env, "tensor", _t)
                        _flush_pend()

                        def _s(s, hi=hi, pt=pt, pb2=pb2, stbk=stbk):
                            _wait(env, s, "act", st["st_mm"][pt])
                            _wait(env, s, "act", env.tok_cst)
                            _wait(env, s, "act", st["pt_free"].get(pt - 2))
                            _bump(env, s.activation(
                                env.ptb[:, 512 * pb2:512 * pb2 + 512],
                                stbk[:, :], AF.Exp,
                                bias=env.cst[:, 2:3]), "s_act")
                            st["exp_done"][pt] = _tok(env, "s_act")
                            st["st_free"][pt] = _tok(env, "s_act")
                        _sec(env, "scalar", _s)

                        def _v(v, pt=pt, pb2=pb2, kt=kt, gu=gu, ub=ub):
                            _wait(env, v, "dve", st["exp_done"][pt])
                            _wait(env, v, "dve", st["m2_load"][gu])
                            _bump(env, v.tensor_tensor(
                                env.ptb[:, 512 * pb2:512 * pb2 + 512],
                                env.ptb[:, 512 * pb2:512 * pb2 + 512],
                                env.m2b[:, 2048 * ub + 512 * kt:
                                        2048 * ub + 512 * kt + 512],
                                ALU.mult), "s_dve")
                            st["pt_mask"][pt] = _tok(env, "s_dve")
                        _sec(env, "vector", _v)

                        def _t2(t, hi=hi, kvh=kvh, kt=kt, gu=gu, ub=ub, pt=pt,
                                pb2=pb2, u=u, nunits=nunits):
                            _wait(env, t, "pe", st["pt_mask"][pt])
                            _wait(env, t, "pe", st["vu_load"][gu])
                            if u == 0 and kt == 0:
                                _wait(env, t, "pe", st["attnT"].get(hi))
                            sbk = env.ps[6 + hi % 2]
                            if kt == 0:
                                _wait(env, t, "pe",
                                      st["srow_ev"].get((hi - 2) % 4))
                                _wait(env, t, "pe", st["srow_ev"].get(hi))
                                if hi % 2 == 0:
                                    _wait(env, t, "pe", st["ps6v"])
                            t.matmul(sbk[0:1, :], env.onescb[:, :],
                                     env.ptb[:, 512 * pb2:512 * pb2 + 512],
                                     start=(kt == 0), stop=(kt == 3))
                            pvb = env.ps[2 + hi]
                            _bump(env, t.matmul(
                                pvb[:, :],
                                env.v_u[:, 2048 * ub + 512 * kvh + 128 * kt:
                                        2048 * ub + 512 * kvh + 128 * kt + 128],
                                env.ptb[:, 512 * pb2:512 * pb2 + 512],
                                start=(u == 0 and kt == 0),
                                stop=(u == nunits - 1 and kt == 3)), "s_pe")
                            st["pt_free"][pt] = _tok(env, "s_pe")
                            if kt == 3:
                                st["srow_mm"][(hi, u)] = _tok(env, "s_pe")
                            if u == nunits - 1 and kt == 3:
                                st["pv_done"][hi] = _tok(env, "s_pe")
                            if hi == 3 and kt == 3:
                                st["ku_use"][gu] = _tok(env, "s_pe")

                        if kt == 3:
                            def _v2(v, hi=hi, u=u):
                                sbk = env.ps[6 + hi % 2]
                                _wait(env, v, "dve", st["srow_mm"][(hi, u)])
                                if u == 0:
                                    _wait(env, v, "dve", t_sz)
                                sl = env.srow[0:1, 512 * hi:512 * hi + 512]
                                _wait(env, v, "dve", st["srow_ev_v"].get(hi))
                                _bump(env, v.tensor_tensor(sl, sl, sbk[0:1, :],
                                                           ALU.add), "s_dve")
                                st["srow_ev"][hi] = _tok(env, "s_dve")
                                st["srow_ev_v"][hi] = _tok(env, "s_dve")
                        else:
                            _v2 = None
                        pend[0] = (_t2, _v2)
            _flush_pend()

            for hi, h in enumerate(heads):
                def _v(v, hi=hi):
                    _wait(env, v, "dve", st["srow_ev_v"].get(hi))
                    _bump(env, v.reciprocal(
                        env.irow[0:1, :],
                        env.srow[0:1, 512 * hi:512 * hi + 512]), "s_dve")
                _sec(env, "vector", _v)
                t_ir = _tok(env, "s_dve")

                def _t(t, hi=hi):
                    _wait(env, t, "pe", t_ir)
                    _wait(env, t, "pe", st["ps6v"])
                    bk = env.ps[6]
                    _bump(env, t.matmul(bk[0:128, :], env.onesr32[:, :],
                                        env.irow[0:1, :], start=True,
                                        stop=True), "s_pe")
                _sec(env, "tensor", _t)
                t_ib = _tok(env, "s_pe")

                def _v2(v, hi=hi, ci=ci):
                    bk = env.ps[6]
                    _wait(env, v, "dve", t_ib)
                    _bump(env, v.tensor_copy(env.invb[:, :], bk[0:128, :]),
                          "s_dve")
                    st["ps6v"] = _tok(env, "s_dve")
                    _wait(env, v, "dve", _tok(env, "s_dve"))
                    _wait(env, v, "dve", st["pv_done"][hi])
                    pvb = env.ps[2 + hi]
                    _bump(env, v.tensor_tensor(
                        env.qa[:, TPC * hi + 512 * ci:TPC * hi + 512 * ci + 512],
                        pvb[:, :], env.invb[:, :], ALU.mult), "s_dve")
                    st["attnT"][hi] = _tok(env, "s_dve")
                _sec(env, "vector", _v2)
            t_attnT = _tok(env, "s_dve")
            st["srow_free"] = t_attnT
            st["attnT_last"] = t_attnT

            hh = q4 // 2
            for n in range(4):
                gw = st["gw"]
                st["gw"] += 1
                wb = gw % 2
                ws = "d_w2A" if wb == 0 else "d_w2B"

                def _ldo(s, n=n, hh=hh, gw=gw, wb=wb, ws=ws):
                    _wait(env, s, "pool", env.tok_cc["misc"])
                    _wait(env, s, "pool", st["wo_use"].get(gw - 2))
                    a = cfg.MISC_WO + 64 * n + 32 * hh
                    segs = _tile_segs(a, a + 32, cfg.MT_PC)
                    ofs = 0
                    for (r, t0, m) in segs:
                        _bump(env, s.dma_start(
                            out=env.w2buf[:, 5632 * wb + 128 * ofs:
                                          5632 * wb + 128 * (ofs + m)],
                            in_=env.wt_misc[128 * r:128 * r + 128,
                                            128 * t0:128 * (t0 + m)]), ws, 16)
                        ofs += m
                    st["wo_load"][gw] = _tok(env, ws)
                _sec(env, "gpsimd", _ldo)

                for tl in range(4):
                    tt = 4 * ci + tl
                    gm = st["gmo"]
                    st["gmo"] += 1
                    bk = env.ps[gm % 2]

                    def _t(t, n=n, tl=tl, q4=q4, ci=ci, gw=gw, wb=wb, gm=gm,
                           bk=bk):
                        _wait(env, t, "pe", t_attnT)
                        _wait(env, t, "pe", st["wo_load"][gw])
                        _wait(env, t, "pe", st["woadd"].get(gm - 2))
                        mm = None
                        for hi in range(4):
                            hpos = 4 * (q4 % 2) + hi
                            mm = t.matmul(bk[:, :],
                                          env.qa[:, TPC * hi + 512 * ci + 128 * tl:
                                                 TPC * hi + 512 * ci + 128 * tl + 128],
                                          env.w2buf[:, 5632 * wb + 512 * hpos:
                                                    5632 * wb + 512 * hpos + 512],
                                          start=(hi == 0), stop=(hi == 3))
                        _bump(env, mm, "s_pe")
                        st["womm"][gm] = _tok(env, "s_pe")
                        if tl == 3:
                            st["wo_use"][gw] = _tok(env, "s_pe")
                    _sec(env, "tensor", _t)

                    def _v(v, n=n, tt=tt, gm=gm, bk=bk):
                        _wait(env, v, "dve", st["womm"][gm])
                        xsl = env.x_sb[:, D * tt + 512 * n:D * tt + 512 * n + 512]
                        _bump(env, v.tensor_tensor(xsl, xsl, bk[:, :], ALU.add),
                              "s_dve")
                        st["woadd"][gm] = _tok(env, "s_dve")
                    _sec(env, "vector", _v)
            st["woadd_last"] = st["woadd"][st["gmo"] - 1]
        for hi in range(4):
            st["qa_free"][(q4, hi)] = _tok(env, "s_pe")

    env.tok_x_ready = st["woadd"][st["gmo"] - 1]
    env.tok_xlt_free = _tok(env, "s_pe")
    env.tok_wq_free = _tok(env, "s_pe")
    fs = env.ffn_state
    fs["w2_use"][fs["gn"] - 1] = _tok(env, "s_pe")
    fs["w2_use"][fs["gn"] - 2] = _tok(env, "s_pe")
    fs["pb_use"][fs["gp"] - 1] = _tok(env, "s_pe")
    fs["pb_use"][fs["gp"] - 2] = _tok(env, "s_pe")
    env.barrier_pe = [_tok(env, "s_pe")]
    env.barrier_dve = [_tok(env, "s_dve")]
    env.barrier_act = [_tok(env, "s_act")]



def _build(cfg):
    nc = bass.Bass("TRN2", target_bir_lowering=False)
    env = Env()
    env.nc = nc
    env.cfg = cfg
    l1 = (cfg.stage == "l1")
    lA = (cfg.stage == "lA")

    x_shard = nc.dram_tensor("x_shard", [2, 512, D], F32, kind="ExternalInput")
    ident_in = nc.dram_tensor("ident", [128, 128], F32, kind="ExternalInput")
    onesc_in = nc.dram_tensor("ones_col", [128, 1], F32, kind="ExternalInput")
    onesr_in = nc.dram_tensor("ones_row", [1, 128], F32, kind="ExternalInput")
    fconst_in = nc.dram_tensor("fconst", [128, 3], F32, kind="ExternalInput")
    out_shard = nc.dram_tensor("out_shard", [2, 512, D], F32, kind="ExternalOutput")

    env.amask2 = None
    if cfg.do_conv:
        conv_src = nc.dram_tensor("conv_src", [cfg.P_A, 2, 64, D], F32,
                                  kind="ExternalInput")
        conv_bv = nc.dram_tensor("conv_bv", [cfg.P_A, 2, 16, 64, 64], F32,
                                 kind="ExternalInput")
        conv_idx = nc.dram_tensor("conv_idx", [128, cfg.P_A], U32,
                                  kind="ExternalInput")
        env.y_dram = nc.dram_tensor("y_dram", [TPC + 128, D], F32)
    if lA:
        T_ALL = cfg.NE * cfg.N_ET + cfg.N_MT
        assert T_ALL % NCORES == 0
        TPCV = T_ALL // NCORES
        cvt_src = nc.dram_tensor("conv_wsrc", [TPCV, 128, 128], F32,
                                 kind="ExternalInput")
        cvt_out = nc.dram_tensor("cvt_out", [128, TPCV * 128], BF16,
                                 kind="ExternalOutput")
    if l1:
        ks_in = nc.dram_tensor("k_scale", [128, 1], F32, kind="ExternalInput")
        env.kvk_in = nc.dram_tensor("kvk_out", [512, TPC], BF16,
                                    kind="ExternalOutput")
        env.kvv_in = nc.dram_tensor("kvv_out", [4096, 128], BF16,
                                    kind="ExternalOutput")
    if cfg.stage == "l2":
        amask2 = nc.dram_tensor("amask2", [2, 8, 4, 128, 512], BF16,
                                kind="ExternalInput")
        qs_in = nc.dram_tensor("q_scale", [128, 1], F32, kind="ExternalInput")
        env.kvk_all = nc.dram_tensor("kvk_all", [2048, TPC], BF16,
                                     kind="ExternalInput")
        env.kvv_all = nc.dram_tensor("kvv_all", [16384, 128], BF16,
                                     kind="ExternalInput")
        env.amask2 = amask2

    if not lA:
        gate_in = nc.dram_tensor("gate_src", [16, D], F32, kind="ExternalInput")
        env.wt_all = [nc.dram_tensor(f"wt_e{e}", [128, cfg.N_ET * 128], F32,
                                     kind="ExternalInput")
                      for e in range(cfg.NE)]
        env.wt_misc = nc.dram_tensor("wt_misc", [128, cfg.N_MT * 128], F32,
                                     kind="ExternalInput")

    with contextlib.ExitStack() as stk:
        env.stack = stk

        def sb(name, shape, dt):
            return stk.enter_context(nc.sbuf_tensor(name, shape, dt))

        env.x_sb = sb("x_sb", [128, NTT * D], F32)
        env.xLT = sb("xLT", [128, NKT * TPC], BF16)
        env.w13buf = sb("w13buf", [128, 8192], BF16)
        env.w2buf = sb("w2buf", [128, 2 * 5632], BF16)
        env.sqs = sb("sqs", [128, D], F32)
        env.xls = sb("xls", [128, D], BF16)
        env.identbf = sb("identbf", [128, 128], BF16)
        env.ident32 = sb("ident32", [128, 128], F32)
        env.onescb = sb("onescb", [128, 1], BF16)
        env.onesc32 = sb("onesc32", [128, 1], F32)
        env.onesr32 = sb("onesr32", [1, 128], F32)
        env.qs2 = sb("qs2", [128, 1], F32)
        env.ks2 = sb("ks2", [128, 1], F32)
        env.gateT = sb("gateT", [128, 256], BF16)
        env.idx_all = sb("idx_all", [128, cfg.P_A], U32)
        env.colw = sb("colw", [128, 40], F32)
        env.gwork = sb("gwork", [128, 48], F32)
        env.combine = sb("combine", [128, 128], F32)
        env.invb = sb("invb", [128, 512], F32)
        env.acol = sb("acol", [128, 24], F32)
        env.mbf = sb("mbf", [128, 1], BF16)
        env.cst = sb("cst", [128, 3], F32)

        AR = 28800
        arena = sb("arena", [128, AR], BF16)

        def carve(lo_part, hi_part, col0, ncols_bf16, dt):
            v = arena[lo_part:hi_part, col0:col0 + ncols_bf16]
            return v if dt == BF16 else v.bitcast(dt)

        env.gstage = carve(0, 16, 4224, 2048, BF16)
        env.craw = carve(0, 128, 4224, 8192, F32)
        env.cblk = carve(0, 128, 12416, 4096, BF16)
        env.bvstage = carve(0, 64, 16512, 4096, BF16)
        env.bvT = carve(0, 128, 20608, 2048, BF16)
        env.scat = carve(0, 128, 22656, 4096, F32)
        env.h_sb = carve(0, 128, 4224, 11264, BF16)
        env.cb = carve(0, 128, 15488, 4096, F32)
        env.silu_t = carve(0, 128, 19584, 2048, F32)
        env.combT = carve(0, 16, 21632, 2048, F32)
        env.qa = carve(0, 128, 4224, 4096, BF16)
        env.kt_u = carve(0, 128, 8320, 4096, BF16)
        env.v_u = carve(0, 128, 12416, 4096, BF16)
        env.ktstage = carve(0, 128, 12416, 4096, BF16)
        env.m2b = carve(0, 128, 16512, 4096, BF16)
        env.vstage = carve(0, 128, 16512, 512, BF16)
        env.ptb = carve(0, 128, 20608, 1024, BF16)
        env.srow = carve(0, 1, 21632, 4096, F32)
        env.irow = carve(0, 1, 25728, 1024, F32)
        env.nrow = carve(0, 1, 26752, 2048, F32)

        env.ps = [stk.enter_context(nc.psum_tensor(f"ps{i}", [128, 512], F32))
                  for i in range(8)]

        env.hdl, env.val, env.waited = {}, {}, {}
        _mk_sems(env, ["d_xs", "d_csrcA", "d_csrcB", "d_bvA", "d_bvB", "d_wt",
                       "d_woA", "d_woB", "d_woC", "d_wtA", "d_wtB", "d_wtC",
                       "d_pbA", "d_pbB", "d_w2A", "d_w2B", "d_y", "d_ylA", "d_ylB", "d_sc",
                       "d_cc", "d_kvw", "d_kuA", "d_kuB", "d_vuA", "d_vuB",
                       "d_m2A", "d_m2B", "d_out", "d_smp", "d_sms",
                       "d_wqA", "d_wqB",
                       "s_pe", "s_dve", "s_act"])
        env.block = stk.enter_context(nc.Block())
        env.rec = {e: Rec() for e in ("tensor", "vector", "scalar", "sync",
                                      "gpsimd")}
        env.barrier_pe = []
        env.barrier_dve = []
        env.barrier_act = []
        env.t_sqs_free = None

        def _c_pool(g):
            _bump(env, g.dma_start(out=env.identbf[:, :], in_=ident_in[:, :]),
                  "d_smp", 16)
            _bump(env, g.dma_start(out=env.onescb[:, :], in_=onesc_in[:, :]),
                  "d_smp", 16)
            if not lA:
                _bump(env, g.dma_start(out=env.gstage[:, :], in_=gate_in[:, :]),
                      "d_smp", 16)
        _sec(env, "gpsimd", _c_pool)
        tok_smp = _tok(env, "d_smp")
        env.tok_smallbf = tok_smp

        def _c_sync(s):
            _bump(env, s.dma_start(out=env.ident32[:, :], in_=ident_in[:, :]),
                  "d_sms", 16)
            _bump(env, s.dma_start(out=env.onesc32[:, :], in_=onesc_in[:, :]),
                  "d_sms", 16)
            _bump(env, s.dma_start(out=env.onesr32[:, :], in_=onesr_in[:, :]),
                  "d_sms", 16)
            if l1:
                _bump(env, s.dma_start(out=env.ks2[:, :], in_=ks_in[:, :]),
                      "d_sms", 16)
            if cfg.do_conv:
                _bump(env, s.dma_start(out=env.idx_all[:, :],
                                       in_=conv_idx[:, :]), "d_sms", 16)
            if cfg.stage == "l2":
                _bump(env, s.dma_start(out=env.qs2[:, :], in_=qs_in[:, :]),
                      "d_sms", 16)
            _bump(env, s.dma_start(out=env.cst[:, :], in_=fconst_in[:, :]),
                  "d_sms", 16)
            for tt in range(NTT):
                ci, r = divmod(tt, 4)
                _bump(env, s.dma_start(out=env.x_sb[:, D * tt:D * tt + D],
                                       in_=x_shard[ci, 128 * r:128 * r + 128, :]),
                      "d_xs", 16)
        _sec(env, "sync", _c_sync)
        tok_sms = _tok(env, "d_sms")
        env.tok_cst = tok_sms

        def _c_dve(v):
            _wait(env, v, "dve", tok_sms)
            if cfg.stage == "l2":
                _bump(env, v.tensor_scalar_mul(env.qs2[:, :], env.qs2[:, :],
                                               1.0 / math.sqrt(HD)), "s_dve")
            else:
                _bump(env, v.engine_nop(), "s_dve")
        _sec(env, "vector", _c_dve)
        env.tok_qs2 = _tok(env, "s_dve")

        gts = []
        for kt in range(NKT if not lA else 0):
            def _t(t, kt=kt):
                _wait(env, t, "pe", tok_smp)
                bk = env.ps[6 + kt % 2]
                _wait(env, t, "pe", gts[-2] if len(gts) >= 2 else None)
                _bump(env, t.transpose(bk[:, :].bitcast(BF16)[0:128, 0:16],
                                       env.gstage[:, 128 * kt:128 * kt + 128],
                                       env.identbf[0:16, 0:16]), "s_pe")
            _sec(env, "tensor", _t)
            t_tr = _tok(env, "s_pe")

            def _v(v, kt=kt):
                bk = env.ps[6 + kt % 2]
                _wait(env, v, "dve", t_tr)
                _bump(env, v.tensor_copy(env.gateT[:, 16 * kt:16 * kt + 16],
                                         bk[:, :].bitcast(BF16)[0:128, 0:16]),
                      "s_dve")
                gts.append(_tok(env, "s_dve"))
            _sec(env, "vector", _v)
        env.tok_gateT = _tok(env, "s_dve") if not lA else None

        env.tok_cc = {e: None for e in range(cfg.NE)}
        env.tok_cc["misc"] = None

        if lA:
            WSC = 16
            cslots = [carve(0, 128, 0, 2048, BF16),
                      carve(0, 128, 2048, 2048, BF16),
                      carve(0, 128, 26752, 2048, BF16)]
            wosem = ["d_woA", "d_woB", "d_woC"]
            wtsem = ["d_wtA", "d_wtB", "d_wtC"]
            nbatch = (TPCV + WSC - 1) // WSC
            cst_state = dict(bi=0, sgc=0, prev={}, evs=[])

            def cvt_step():
                bi = cst_state["bi"]
                if bi >= nbatch:
                    return
                cst_state["bi"] += 1
                t0 = WSC * bi
                cnt = min(WSC, TPCV - t0)
                slot = cslots[bi % 3]

                def _ld(g, bi=bi, t0=t0, cnt=cnt, slot=slot):
                    _wait(env, g, "pool", cst_state["prev"].get(bi - 3))
                    _bump(env, g.dma_start(
                        out=slot[:, 0:128 * cnt].rearrange(
                            "p (t c) -> p t c", t=cnt),
                        in_=cvt_src[t0:t0 + cnt].rearrange("t p c -> p t c")),
                        wtsem[bi % 3], 16)
                _sec(env, "gpsimd", _ld)
                t_ld = _tok(env, wtsem[bi % 3])

                for lo in range(0, cnt, 8):
                    hi = min(lo + 8, cnt)
                    sg = cst_state["sgc"]
                    cst_state["sgc"] += 1
                    bk = env.ps[6 + sg % 2]

                    def _t(t, lo=lo, hi=hi, sg=sg, bk=bk, t_ld=t_ld,
                           slot=slot):
                        _wait(env, t, "pe", tok_smp)
                        _wait(env, t, "pe", t_ld)
                        _wait(env, t, "pe",
                              cst_state["evs"][sg - 2] if sg >= 2 else None)
                        for jj in range(lo, hi):
                            _bump(env, t.transpose(
                                bk[:, :].bitcast(BF16)[:, 128 * (jj - lo):
                                                       128 * (jj - lo) + 128],
                                slot[:, 128 * jj:128 * jj + 128],
                                env.identbf[:, :]), "s_pe")
                    _sec(env, "tensor", _t)
                    t_tr = _tok(env, "s_pe")

                    def _v(v, lo=lo, hi=hi, bk=bk, t_tr=t_tr, slot=slot):
                        _wait(env, v, "dve", t_tr)
                        _bump(env, v.tensor_copy(
                            slot[:, 128 * lo:128 * hi],
                            bk[:, :].bitcast(BF16)[:, 0:128 * (hi - lo)]),
                            "s_dve")
                        cst_state["evs"].append(_tok(env, "s_dve"))
                    _sec(env, "vector", _v)
                t_ev = _tok(env, "s_dve")

                def _out(s, bi=bi, t0=t0, cnt=cnt, t_ev=t_ev, slot=slot):
                    _wait(env, s, "act", t_ev)
                    _bump(env, s.dma_start(
                        out=cvt_out[:, 128 * t0:128 * (t0 + cnt)],
                        in_=slot[:, 0:128 * cnt]), wosem[bi % 3], 16)
                _sec(env, "scalar", _out)
                cst_state["prev"][bi] = _tok(env, wosem[bi % 3])

            env.cvt_step = cvt_step
            env.cvt_nbatch = nbatch
        else:
            env.cvt_step = lambda: None
            env.cvt_nbatch = 0

        env.tok_x_ready = _tok(env, "d_xs")
        if cfg.do_conv:
            def _z(v):
                _bump(env, v.memset(env.scat[:, 0:512], 0.0), "s_dve")
            _sec(env, "vector", _z)
            t_z = _tok(env, "s_dve")

            def _zs(s):
                _wait(env, s, "sync", t_z)
                for rt in range(NTT):
                    for cq in range(4):
                        _bump(env, s.dma_start(
                            out=env.y_dram[128 * rt:128 * rt + 128,
                                           512 * cq:512 * cq + 512],
                            in_=env.scat[:, 0:512]), "d_y", 16)
            _sec(env, "sync", _zs)
            t_yz = _tok(env, "d_y")

            ln_done = {}
            bvt_pe = {}
            blk_use = {}
            bvT_use = {}
            mm_tok = {}
            sc_ev = {}
            scat_done = {}
            for sl in range(cfg.P_A):
                for _ in range(3):
                    env.cvt_step()
                buf = sl % 2
                cs = "d_csrcA" if buf == 0 else "d_csrcB"
                bs = "d_bvA" if buf == 0 else "d_bvB"

                def _ld(s, sl=sl, buf=buf, cs=cs):
                    _wait(env, s, "sync", ln_done.get(sl - 2))
                    if sl < 2:
                        _wait(env, s, "sync", env.tok_gateT)
                    _bump(env, s.dma_start(
                        out=env.craw[0:64, D * buf:D * buf + D],
                        in_=conv_src[sl, 0]), cs, 16)
                    _bump(env, s.dma_start(
                        out=env.craw[64:128, D * buf:D * buf + D],
                        in_=conv_src[sl, 1]), cs, 16)
                _sec(env, "sync", _ld)
                t_cs = _tok(env, cs)

                def _ldb(g, sl=sl, buf=buf, bs=bs):
                    _wait(env, g, "pool", bvt_pe.get(sl - 2))
                    _bump(env, g.dma_start(
                        out=env.bvstage[:, 2048 * buf:
                                        2048 * buf + 1024].rearrange(
                                            "p (k j) -> p k j", k=16),
                        in_=conv_bv[sl, 0].rearrange("k i j -> i k j")), bs, 16)
                    _bump(env, g.dma_start(
                        out=env.bvstage[:, 2048 * buf + 1024:
                                        2048 * buf + 2048].rearrange(
                                            "p (k j) -> p k j", k=16),
                        in_=conv_bv[sl, 1].rearrange("k i j -> i k j")), bs, 16)
                _sec(env, "gpsimd", _ldb)
                t_bs = _tok(env, bs)

                def _lnsq(s, sl=sl, buf=buf, t_cs=t_cs):
                    _wait(env, s, "act", t_cs)
                    _wait(env, s, "act", env.t_sqs_free)
                    _bump(env, s.activation(env.sqs[:, :],
                                            env.craw[:, D * buf:D * buf + D],
                                            AF.Square), "s_act")
                _sec(env, "scalar", _lnsq)
                t_sq0 = _tok(env, "s_act")

                def _ln(v, sl=sl, buf=buf, t_cs=t_cs, t_sq0=t_sq0):
                    _wait(env, v, "dve", t_cs)
                    _wait(env, v, "dve", blk_use.get(sl - 2))
                    cw = env.colw
                    xt = env.craw[:, D * buf:D * buf + D]
                    _bump(env, v.reduce_sum(cw[:, 24:25], xt, axis=AX.X), "s_dve")
                    _wait(env, v, "dve", t_sq0)
                    _bump(env, v.reduce_sum(cw[:, 25:26], env.sqs[:, :],
                                            axis=AX.X), "s_dve")
                    env.t_sqs_free = _tok(env, "s_dve")
                    _wait(env, v, "dve", _tok(env, "s_dve"))
                    _bump(env, v.tensor_tensor(cw[:, 26:27], cw[:, 24:25],
                                               cw[:, 24:25], ALU.mult), "s_dve")
                    _wait(env, v, "dve", _tok(env, "s_dve"))
                    _bump(env, v.scalar_tensor_tensor(cw[:, 27:28], cw[:, 26:27],
                                                      -1.0 / D, cw[:, 25:26],
                                                      ALU.mult, ALU.add),
                          "s_dve")
                _sec(env, "vector", _ln)
                t_v1 = _tok(env, "s_dve")

                def _lns(s, sl=sl):
                    _wait(env, s, "act", t_v1)
                    _wait(env, s, "act", env.tok_cst)
                    _bump(env, s.activation(env.colw[:, 28:29], env.colw[:, 27:28],
                                            AF.Sqrt, bias=env.cst[:, 0:1],
                                            scale=1.0 / D), "s_act")
                _sec(env, "scalar", _lns)
                t_sq = _tok(env, "s_act")

                def _ln2(v, sl=sl, buf=buf):
                    cw = env.colw
                    _wait(env, v, "dve", t_sq)
                    _bump(env, v.reciprocal(cw[:, 29:30], cw[:, 28:29]), "s_dve")
                    _wait(env, v, "dve", _tok(env, "s_dve"))
                    _bump(env, v.tensor_scalar_mul(cw[:, 30:31], cw[:, 24:25],
                                                   1.0 / D), "s_dve")
                    _wait(env, v, "dve", _tok(env, "s_dve"))
                    _bump(env, v.tensor_scalar(env.cblk[:, D * buf:D * buf + D],
                                               env.craw[:, D * buf:D * buf + D],
                                               cw[:, 30:31], cw[:, 29:30],
                                               ALU.subtract, ALU.mult), "s_dve")
                    ln_done[sl] = _tok(env, "s_dve")
                _sec(env, "vector", _ln2)
                t_blk = _tok(env, "s_dve")

                def _bt(t, sl=sl, buf=buf, t_bs=t_bs):
                    _wait(env, t, "pe", t_bs)
                    _wait(env, t, "pe", tok_smp)
                    _wait(env, t, "pe", bvT_use.get(sl - 2))
                    _wait(env, t, "pe", sc_ev.get(("bv", sl - 2)))
                    bkA, bkB = env.ps[4], env.ps[5]
                    for k in range(16):
                        _bump(env, t.transpose(
                            bkA[:, :].bitcast(BF16)[0:64, 64 * k:64 * k + 64],
                            env.bvstage[:, 2048 * buf + 64 * k:
                                        2048 * buf + 64 * k + 64],
                            env.identbf[0:64, 0:64]), "s_pe")
                        _bump(env, t.transpose(
                            bkB[:, :].bitcast(BF16)[64:128, 64 * k:64 * k + 64],
                            env.bvstage[:, 2048 * buf + 1024 + 64 * k:
                                        2048 * buf + 1024 + 64 * k + 64],
                            env.identbf[0:64, 0:64],
                            tile_position=(0, 64)), "s_pe")
                    bvt_pe[sl] = _tok(env, "s_pe")
                _sec(env, "tensor", _bt)
                t_bt = _tok(env, "s_pe")

                def _bev(v, sl=sl, buf=buf):
                    _wait(env, v, "dve", t_bt)
                    bkA, bkB = env.ps[4], env.ps[5]
                    _bump(env, v.tensor_copy(
                        env.bvT[0:64, 1024 * buf:1024 * buf + 1024],
                        bkA[:, :].bitcast(BF16)[0:64, 0:1024]), "s_dve")
                    _bump(env, v.tensor_copy(
                        env.bvT[64:128, 1024 * buf:1024 * buf + 1024],
                        bkB[:, :].bitcast(BF16)[64:128, 0:1024]), "s_dve")
                    sc_ev[("bv", sl)] = _tok(env, "s_dve")
                _sec(env, "vector", _bev)
                t_bvT = _tok(env, "s_dve")

                for chq in range(4):
                    cu = 4 * sl + chq
                    pA = env.ps[0 if chq % 2 == 0 else 2]
                    pB = env.ps[1 if chq % 2 == 0 else 3]

                    def _cv(t, sl=sl, buf=buf, chq=chq, cu=cu, pA=pA, pB=pB,
                            t_blk=t_blk, t_bvT=t_bvT):
                        _wait(env, t, "pe", t_blk)
                        _wait(env, t, "pe", t_bvT)
                        _wait(env, t, "pe", sc_ev.get(("sc", cu - 2)))
                        first = True
                        mm = None
                        for k in range(15, -1, -1):
                            c0 = 512 * chq + k - 15
                            if c0 < 0:
                                w = 512 + c0
                                po = -c0
                                c0 = 0
                            else:
                                w, po = 512, 0
                            lA = env.bvT[0:64, 1024 * buf + 64 * k:
                                         1024 * buf + 64 * k + 64]
                            lB = env.bvT[64:128, 1024 * buf + 64 * k:
                                         1024 * buf + 64 * k + 64]
                            t.ldweights(lA, tile_position=(0, 0))
                            mmA = t.matmul(pA[0:64, po:po + w], lA,
                                           env.cblk[0:64, D * buf + c0:
                                                    D * buf + c0 + w],
                                           start=first, stop=(k == 0),
                                           tile_position=(0, 0))
                            t.ldweights(lB, tile_position=(64, 64))
                            mm = t.matmul(pB[64:128, po:po + w], lB,
                                          env.cblk[64:128, D * buf + c0:
                                                   D * buf + c0 + w],
                                          start=first, stop=(k == 0),
                                          tile_position=(64, 64))
                            first = False
                        _bump(env, mmA, "s_pe")
                        _bump(env, mm, "s_pe")
                        mm_tok[cu] = _tok(env, "s_pe")
                        if chq == 3:
                            blk_use[sl] = _tok(env, "s_pe")
                            bvT_use[sl] = _tok(env, "s_pe")
                    _sec(env, "tensor", _cv)

                    def _ev(v, sl=sl, chq=chq, cu=cu, pA=pA, pB=pB):
                        _wait(env, v, "dve", mm_tok[cu])
                        _wait(env, v, "dve", t_yz)
                        _wait(env, v, "dve", scat_done.get(sl - 1))
                        _bump(env, v.tensor_copy(
                            env.scat[0:64, 512 * chq:512 * chq + 512],
                            pA[0:64, :]), "s_dve")
                        _bump(env, v.tensor_copy(
                            env.scat[64:128, 512 * chq:512 * chq + 512],
                            pB[64:128, :]), "s_dve")
                        sc_ev[("sc", cu)] = _tok(env, "s_dve")
                    _sec(env, "vector", _ev)
                    t_evs = _tok(env, "s_dve")

                    if chq == 3:
                        def _scat(g, sl=sl, t_evs=t_evs):
                            _wait(env, g, "gpsimd", t_evs)
                            _wait(env, g, "gpsimd", t_yz)
                            _wait(env, g, "gpsimd", tok_sms)
                            _wait(env, g, "gpsimd",
                                  ("d_sc", env.val["d_sc"]))
                            _bump(env, g.indirect_dma_start(
                                out=env.y_dram[:, :],
                                out_offset=bass.IndirectOffsetOnAxis(
                                    ap=env.idx_all[:, sl:sl + 1], axis=0),
                                in_=env.scat[:, :],
                                in_offset=None,
                                compute_op=ALU.add), "d_sc", 16)
                            scat_done[sl] = _tok(env, "d_sc")
                        _sec(env, "gpsimd", _scat)

            t_allscat = _tok(env, "d_sc")
            yadds = []
            for tt in range(NTT):
                env.cvt_step()
                buf = tt % 2

                ysem = "d_ylA" if buf == 0 else "d_ylB"

                def _ldy(s, tt=tt, buf=buf, ysem=ysem):
                    _wait(env, s, "sync", t_allscat)
                    _wait(env, s, "sync", ln_done.get(cfg.P_A - 1))
                    _wait(env, s, "sync", ln_done.get(cfg.P_A - 2))
                    _wait(env, s, "sync", yadds[-2] if len(yadds) >= 2 else None)
                    _bump(env, s.dma_start(
                        out=env.craw[:, D * buf:D * buf + D],
                        in_=env.y_dram[128 * tt:128 * tt + 128, :]), ysem, 16)
                _sec(env, "sync", _ldy)
                t_ld = _tok(env, ysem)

                def _add(v, tt=tt, buf=buf):
                    _wait(env, v, "dve", t_ld)
                    _wait(env, v, "dve", ("d_xs", env.val["d_xs"]))
                    xt = env.x_sb[:, D * tt:D * tt + D]
                    _bump(env, v.tensor_tensor(
                        xt, xt, env.craw[:, D * buf:D * buf + D], ALU.add),
                        "s_dve")
                    yadds.append(_tok(env, "s_dve"))
                _sec(env, "vector", _add)
            for _ in range(env.cvt_nbatch):
                env.cvt_step()
            env.tok_x_ready = _tok(env, "s_dve")
            env.barrier_pe = [_tok(env, "s_pe")]
            env.barrier_dve = [_tok(env, "s_dve"), ("d_sc", env.val["d_sc"]),
                              ("d_y", env.val["d_y"])]
        else:
            def _w(v):
                _wait(env, v, "dve", ("d_xs", env.val["d_xs"]))
                _bump(env, v.engine_nop(), "s_dve")
            _sec(env, "vector", _w)
            env.tok_x_ready = _tok(env, "s_dve")
            env.barrier_pe = [_tok(env, "s_pe")]
            env.barrier_dve = [_tok(env, "s_dve")]

        env.tok_xlt_free = None
        env.t_xls_free = None
        env.t_sqs_free = None
        env.tok_h_free = None
        env.tok_wq_free = None
        env.ffn_state = dict(gp=0, gu=0, gn=0, gm=0, pb_load={}, pb_use={},
                             unit_mm={}, silu_done={}, op1_done={},
                             w2_load={}, w2_use={}, mm2_done={}, xadd_done={})
        env.att_state = dict(gu=0, pt=0, gw=0, gmo=0, wq_load={}, wq_use={},
                             qa_free={}, ku_load={}, ku_use={}, vu_load={},
                             m2_load={}, p1mm={}, st_mm={}, st_free={},
                             exp_done={}, pt_mask={}, pt_free={}, srow_mm={},
                             srow_ev={}, srow_ev_v={}, srow_free=None,
                             pv_done={}, attnT={}, wo_load={}, wo_use={},
                             womm={}, woadd={}, woadd_last=None, ps6v=None, attnT_last=None, qguard={},
                             w2_free=None)

        if lA:
            t_final = env.tok_x_ready
        elif l1:
            _emit_ln(env)
            _emit_ffn(env, 0)
            t_final = env.tok_x_ready
            env.att_state["w2_free"] = env.ffn_state["w2_use"].get(
                env.ffn_state["gn"] - 1)
            _emit_ln(env)
            _emit_attn_kv(env)
        else:
            env.tok_kv = None
            _emit_ln(env)
            _emit_attn_core(env)
            _emit_ln(env)
            _emit_ffn(env, 1)
            t_final = env.tok_x_ready

        tok_ci0 = getattr(env, "tok_x_ci0", None) if not lA else None

        def _out(s):
            for wave in range(2):
                tok = tok_ci0 if (wave == 0 and tok_ci0 is not None)                     else t_final
                _wait(env, s, "sync", tok)
                for tt in range(4 * wave, 4 * wave + 4):
                    ci, r = divmod(tt, 4)
                    _bump(env, s.dma_start(
                        out=out_shard[ci, 128 * r:128 * r + 128, :],
                        in_=env.x_sb[:, D * tt:D * tt + D]), "d_out", 16)
        _sec(env, "sync", _out)

        def _fin(g):
            g.wait_ge(env.hdl["d_out"], env.val["d_out"])
            if l1:
                g.wait_ge(env.hdl["d_kvw"], env.val["d_kvw"])
            if lA:
                for sname in ("d_woA", "d_woB", "d_woC",
                              "d_wtA", "d_wtB", "d_wtC"):
                    if env.val[sname]:
                        g.wait_ge(env.hdl[sname], env.val[sname])
        _sec(env, "gpsimd", _fin)

        _replay_all(env)

    return nc



def kernel(L0_attn_a_x, L0_attn_a_block_values, L0_attn_a_cols, L0_attn_a_rows,
           L0_attn_b_Wq, L0_attn_b_Wk, L0_attn_b_Wv, L0_attn_b_Wo,
           L0_attn_b_q_scale, L0_attn_b_k_scale, L0_ffn_gate_w, L0_ffn_w13,
           L0_ffn_w2, L0_ffn_sw13, L0_ffn_sw2):
    x = np.asarray(L0_attn_a_x, np.float32)
    bv = np.asarray(L0_attn_a_block_values, np.float32)
    cols = np.asarray(L0_attn_a_cols).astype(np.int64)
    rows = np.asarray(L0_attn_a_rows).astype(np.int64)
    plan = _plan(rows)
    cfgA = Cfg(P_A=plan[2], stage="lA")
    cfg1 = Cfg(P_A=plan[2], stage="l1")
    cfg2 = Cfg(P_A=plan[2], stage="l2", do_conv=False)
    staged = []
    for c in range(NCORES):
        staged.append(_stage_core(
            cfgA, c, plan, x, bv, cols, rows,
            np.asarray(L0_attn_b_Wq, np.float32),
            np.asarray(L0_attn_b_Wk, np.float32),
            np.asarray(L0_attn_b_Wv, np.float32),
            np.asarray(L0_attn_b_Wo, np.float32),
            L0_attn_b_q_scale, L0_attn_b_k_scale,
            np.asarray(L0_ffn_gate_w, np.float32),
            np.asarray(L0_ffn_w13, np.float32),
            np.asarray(L0_ffn_w2, np.float32),
            np.asarray(L0_ffn_sw13, np.float32),
            np.asarray(L0_ffn_sw2, np.float32)))

    w13 = np.asarray(L0_ffn_w13, np.float32)
    w2 = np.asarray(L0_ffn_w2, np.float32)
    named = {"sw13": np.asarray(L0_ffn_sw13, np.float32),
             "sw2": np.asarray(L0_ffn_sw2, np.float32),
             "Wq": np.asarray(L0_attn_b_Wq, np.float32),
             "Wk": np.asarray(L0_attn_b_Wk, np.float32),
             "Wv": np.asarray(L0_attn_b_Wv, np.float32),
             "Wo": np.asarray(L0_attn_b_Wo, np.float32)}
    wt_in = {}
    for e in range(cfgA.NE):
        buf = np.empty((128, 128 * cfgA.N_ET), np.float32)
        for i in range(cfgA.N_ET):
            kind, r0, c0 = _expert_tile_src(cfgA, i)
            a = w13[e] if kind == "w13" else w2[e]
            buf[:, 128 * i:128 * i + 128] = a[r0:r0 + 128, c0:c0 + 128].T
        wt_in[f"wt_e{e}"] = buf
    mbuf = np.empty((128, 128 * cfgA.N_MT), np.float32)
    for i in range(cfgA.N_MT):
        kind, r0, c0 = _misc_tile_src(cfgA, i)
        a = named[kind]
        mbuf[:, 128 * i:128 * i + 128] = a[r0:r0 + 128, c0:c0 + 128].T
    wt_in["wt_misc"] = mbuf

    keys1 = ["x_shard", "conv_src", "conv_bv", "conv_idx", "ident", "ones_col",
             "ones_row", "k_scale", "gate_src", "fconst"]
    keys2 = ["amask2", "ident", "ones_col", "ones_row", "q_scale", "gate_src",
             "fconst"]

    nc1 = _build(cfg1)
    in1 = [{**{k: staged[c][k] for k in keys1}, **wt_in}
           for c in range(NCORES)]
    res1 = run_bass_kernel_spmd(nc1, in1, core_ids=list(range(NCORES)))
    x2 = [np.ascontiguousarray(np.asarray(res1.results[c]["out_shard"]))
          for c in range(NCORES)]
    kvk = [np.asarray(res1.results[c]["kvk_out"]) for c in range(NCORES)]
    kvv = [np.asarray(res1.results[c]["kvv_out"]) for c in range(NCORES)]
    kvk_g = [np.ascontiguousarray(np.concatenate(kvk[4 * g:4 * g + 4], axis=0))
             for g in range(2)]
    kvv_g = [np.ascontiguousarray(np.concatenate(kvv[4 * g:4 * g + 4], axis=0))
             for g in range(2)]

    nc2 = _build(cfg2)
    in2 = []
    for c in range(NCORES):
        m = {k: staged[c][k] for k in keys2}
        m.update(wt_in)
        m["x_shard"] = x2[c]
        m["kvk_all"] = kvk_g[c // 4]
        m["kvv_all"] = kvv_g[c // 4]
        in2.append(m)
    res2 = run_bass_kernel_spmd(nc2, in2, core_ids=list(range(NCORES)))

    out = np.empty((B, T, D), np.float32)
    for c, r in enumerate(res2.results):
        b, j = divmod(c, 4)
        sh = np.asarray(r["out_shard"])
        out[b, 512 * j:512 * j + 512] = sh[0]
        out[b, 512 * (7 - j):512 * (7 - j) + 512] = sh[1]
    return out
